# version 38
# baseline (speedup 1.0000x reference)
import contextlib
import numpy as np
import ml_dtypes
import concourse.bass as bass
import concourse.mybir as mybir
from concourse.bass_utils import run_bass_kernel_spmd

F32 = mybir.dt.float32
BF16 = mybir.dt.bfloat16
AF = mybir.ActivationFunctionType
ALU = mybir.AluOpType

BF = ml_dtypes.bfloat16

B, SEQ, D = 4, 4096, 2048
HEADS, HD, CH = 16, 128, 128
TPC = 2048
NT, TT = 4, 512
KC = 16
NG, GH = 4, 4
NCH = 16
H1, MC = 8192, 64
EPS_LN, EPS_RMS = 1e-5, 1e-6
RELAX_SAME_ENGINE_WAR = False


class Sched:
    ENGS = ("pe", "act", "dve", "pool", "sp")

    def __init__(self):
        self.step = {"pe": 1, "act": 1, "dve": 1, "pool": 1, "coll": 1}
        self.cnt = {e: 0 for e in self.step}
        self.queues = {e: [] for e in Sched.ENGS}
        self.res = {}
        self.wm = {}
        self.pend = {e: [] for e in self.step}

    def sem(self, name, step=16):
        if name not in self.step:
            self.step[name] = step
            self.cnt[name] = 0
            self.pend[name] = []
        return name

    def _r(self, k):
        if k not in self.res:
            self.res[k] = {"w": None, "r": {}}
        return self.res[k]

    def emit(self, eng, fn, reads=(), writes=(), inc=True, sem="same"):
        need = {}

        def _need(t, c, war=False):
            if t == "pe" and eng == "pe":
                return
            if war and t == eng and RELAX_SAME_ENGINE_WAR:
                return
            if need.get(t, 0) < c:
                need[t] = c

        for k in reads:
            r = self.res.get(k)
            if r and r["w"]:
                _need(*r["w"])
        for k in writes:
            r = self.res.get(k)
            if r:
                if r["w"]:
                    _need(*r["w"], war=True)
                for t, c in r["r"].items():
                    _need(t, c, war=True)
        for t, c in sorted(need.items()):
            if self.wm.get((eng, t), 0) >= c:
                continue
            self.queues[eng].append(("wait", t, c))
            self.wm[(eng, t)] = c
        semname = eng if sem == "same" else sem
        if inc:
            self.cnt[semname] += self.step[semname]
            cafter = self.cnt[semname]
            self.queues[eng].append(("inst", fn, semname, self.step[semname]))
            for k, isw in self.pend[semname]:
                r = self._r(k)
                if isw:
                    r["w"] = (semname, cafter)
                    r["r"] = {}
                else:
                    if r["r"].get(semname, 0) < cafter:
                        r["r"][semname] = cafter
            self.pend[semname] = []
            for k in writes:
                r = self._r(k)
                r["w"] = (semname, cafter)
                r["r"] = {}
            for k in reads:
                r = self._r(k)
                if r["r"].get(semname, 0) < cafter:
                    r["r"][semname] = cafter
        else:
            self.queues[eng].append(("inst", fn, None, 0))
            for k in writes:
                self.pend[semname].append((k, True))
            for k in reads:
                self.pend[semname].append((k, False))

    WCAST_SEMS = tuple(["wc0", "wc1"] + [f"wc2{i}" for i in range(16)])

    def phase_barrier(self, exclude=WCAST_SEMS):
        for e in Sched.ENGS:
            for t in list(self.cnt):
                c = self.cnt[t]
                if c == 0 or t == e or t in exclude:
                    continue
                if self.wm.get((e, t), 0) >= c:
                    continue
                self.queues[e].append(("wait", t, c))
                self.wm[(e, t)] = c

    def barrier_wait(self, eng, tgt, cnt):
        if cnt <= 0 or self.wm.get((eng, tgt), 0) >= cnt:
            return
        self.queues[eng].append(("wait", tgt, cnt))
        self.wm[(eng, tgt)] = cnt

    def bump_writes_to_total(self, keys, sem):
        for k in keys:
            r = self.res.get(k)
            if r and r["w"] and r["w"][0] == sem:
                r["w"] = (sem, self.cnt[sem])


def _tables():
    j = np.arange(HD // 2, dtype=np.float32)
    theta = (10000.0 ** (-2.0 * j / HD)).astype(np.float32)
    lam = -(2.0 ** (-8.0 * (np.arange(HEADS, dtype=np.float32) + 1.0) / HEADS))

    cc, ss = [], []
    for half in range(2):
        pos = np.arange(half * TPC, (half + 1) * TPC, dtype=np.float32)
        ang = pos[None, :] * theta[:, None]
        c = np.cos(ang).astype(np.float32)
        s = np.sin(ang).astype(np.float32)
        cc.append(np.ascontiguousarray(np.concatenate([c, c], axis=0)))
        ss.append(np.ascontiguousarray(np.concatenate([-s, s], axis=0)))

    i = np.arange(CH, dtype=np.float32)
    diff = i[None, :] - i[:, None]
    dt = np.zeros((CH, HEADS * CH), np.float32)
    dkt = np.zeros((CH, HEADS * CH), np.float32)
    dq = np.zeros((CH, HEADS * CH), np.float32)
    for h in range(HEADS):
        m = np.where(diff >= 0, np.exp(lam[h] * np.maximum(diff, 0.0)), 0.0)
        dt[:, h * CH:(h + 1) * CH] = m
        dkt[:, h * CH:(h + 1) * CH] = np.exp(lam[h] * (CH - 1.0 - i))[:, None]
        dq[:, h * CH:(h + 1) * CH] = np.exp(lam[h] * (i + 1.0))[None, :]
    p = np.arange(TPC, dtype=np.float32)
    tabg = np.zeros((HEADS * HD, TPC), np.float32)
    for h in range(HEADS):
        tabg[h * HD:(h + 1) * HD, :] = np.exp(lam[h] * (p + 1.0))[None, :]
    lam_c = [float(np.exp(np.float64(lam[h]) * CH)) for h in range(HEADS)]
    return cc, ss, dt.astype(BF), dkt.astype(BF), dq.astype(BF), tabg.astype(BF), lam_c


def build_program(debug=False):
    nc = bass.Bass("TRN2", target_bir_lowering=False)

    def din(name, shape, dt=F32):
        return nc.dram_tensor(name, shape, dt, kind="ExternalInput").ap()

    x_fm = din("x_fm", [D, TPC])
    wq = din("wq", [D, D]); wk = din("wk", [D, D]); wv = din("wv", [D, D])
    wo = din("wo", [D, D])
    wg1 = din("wg1", [D, HD]); wg2 = din("wg2", [HD, D])
    nw_col = din("nw_col", [HD, HEADS])
    p1t = din("p1t", [D, H1]); b1_col = din("b1_col", [HD, MC])
    p2t = din("p2t", [KC * H1, HD]); b2_col = din("b2_col", [HD, KC])
    cc_t = din("cc_t", [HD, TPC]); ss_t = din("ss_t", [HD, TPC])
    dt_t = din("dt_t", [CH, HEADS * CH], BF16)
    dkt_t = din("dkt_t", [CH, HEADS * CH], BF16)
    dq_t = din("dq_t", [CH, HEADS * CH], BF16)
    ones_in = din("ones_in", [HD, HD])
    ident_in = din("ident_in", [HD, HD])
    sel_in = din("sel_in", [HD, 1])
    cmask_in = din("cmask_in", [HD, 1])
    epsc_in = din("epsc_in", [HD, 2])
    gate_w_in = din("gate_w_in", [D, 8])

    y = nc.dram_tensor("y", [D, TPC], F32, kind="ExternalOutput").ap()

    xn_d = nc.dram_tensor("xn_d", [D, TPC], BF16, kind="Internal").ap()
    qdg_d = nc.dram_tensor("qdg_d", [NG * NCH * HD, GH * CH], BF16, kind="Internal").ap()
    oraw_d = nc.dram_tensor("oraw_d", [NG * NCH * HD, GH * CH], BF16, kind="Internal").ap()
    s_d = nc.dram_tensor("s_d", [D, TPC], BF16, kind="Internal").ap()
    wo_bf_d = nc.dram_tensor("wo_bf_d", [D, D], BF16, kind="Internal").ap()
    p1t_bf_d = nc.dram_tensor("p1t_bf_d", [D, H1], BF16, kind="Internal").ap()
    p2t_bf_d = nc.dram_tensor("p2t_bf_d", [KC * H1, HD], BF16, kind="Internal").ap()
    cc_in_d = nc.dram_tensor("cc_in_d", [HEADS * HD, HD], F32, kind="Internal").ap()
    cc_out_d = nc.dram_tensor("cc_out_d", [HEADS * HD, HD], F32, kind="Internal").ap()

    lam_c = _tables()[-1]

    sch = Sched()

    with contextlib.ExitStack() as ctx:
        RA = ctx.enter_context(nc.sbuf_tensor("RA", [128, 16384], F32))
        RB = ctx.enter_context(nc.sbuf_tensor("RB", [128, 12288], F32))
        RC = ctx.enter_context(nc.sbuf_tensor("RC", [128, 12288], F32))
        RD = ctx.enter_context(nc.sbuf_tensor("RD", [128, 6144], F32))
        RE = ctx.enter_context(nc.sbuf_tensor("RE", [128, 5888], F32))

        def f32(reg, off, n):
            return reg[:, off:off + n]

        def bf16(reg, off_f32, n_bf):
            return reg[:, off_f32:off_f32 + n_bf // 2].bitcast(BF16)

        pb = [ctx.enter_context(nc.psum_tensor(f"pb{i}", [128, 512], F32)) for i in range(6)]
        pbf = [ctx.enter_context(nc.psum_tensor(f"pbf{i}", [128, 512], BF16)) for i in range(2)]

        sems = {}
        for e in ("pe", "act", "dve", "pool", "coll"):
            sems[e] = ctx.enter_context(nc.semaphore(f"sem_{e}"))

        blk = ctx.enter_context(nc.Block())

        g1_bf = bf16(RD, 0, 2048)
        wg1_bf = bf16(RD, 1024, 2048)
        wg2_bf = bf16(RD, 2048, 2048)
        s_all = f32(RD, 3072, 2048)
        ones_bf = bf16(RD, 5120, 128)
        ident_bf = bf16(RD, 5184, 128)
        sel_col = f32(RD, 5248, 1)
        cmask_col = f32(RD, 5249, 1)
        epsc = f32(RD, 5250, 2)
        nw_c = f32(RD, 5252, 16)
        b2_c = f32(RD, 5268, 16)
        b1_c = f32(RD, 5284, 64)
        sbf_ring = [bf16(RD, 5348 + 256 * i, 512) for i in range(2)]
        gw_bf = bf16(RD, 5860, 128)

        cc_sb = f32(RC, 0, 2048)
        ss_sb = f32(RC, 2048, 2048)
        wslab2 = [bf16(RC, 4096 + 4096 * i, 8192) for i in range(2)]
        dt_sb = bf16(RB, 9216, 2048)
        dkt_sb = bf16(RB, 10240, 2048)
        dq_sb = bf16(RB, 11264, 2048)
        wo_slab = [bf16(RC, i * 4096, 8192) for i in range(2)]
        p1_slab = [bf16(RC, i * 2048, 4096) for i in range(2)]
        p2_slab = [bf16(RC, 4096 + i * 4096, 8192) for i in range(2)]

        qs_ring = [bf16(RE, 256 * i, 512) for i in range(4)]
        qw_ring = [bf16(RE, 1024 + 256 * i, 512) for i in range(4)]
        sd_ring = [bf16(RE, 2048 + 256 * i, 512) for i in range(2)]
        kd_ring = [bf16(RE, 2560 + 256 * i, 512) for i in range(2)]
        qd_ring = [bf16(RE, 3072 + 256 * i, 512) for i in range(2)]
        oraw_stage = [bf16(RE, 3584 + i * 256, 512) for i in range(2)]
        qdg_stage = [bf16(RE, 4096 + i * 256, 512) for i in range(2)]
        a_t2r = [f32(RE, 3584 + 512 * i, 512) for i in range(2)]
        m_row = f32(RE, 0, 512)[0:1, :]
        msq_row = f32(RE, 512, 512)[0:1, :]
        var_row = f32(RE, 1024, 512)[0:1, :]
        ln_row = f32(RE, 1536, 512)[0:1, :]
        rstd_f = f32(RE, 2048, 512)[0:1, :]
        rstd_bf_row = bf16(RE, 2560, 512)[0:1, :]
        nmr_bf_row = bf16(RE, 2816, 512)[0:1, :]
        r2_sb = f32(RE, 3072, 512)
        s_prev_bf = bf16(RE, 4608, 2048)
        rstd_rows = bf16(RE, 0, 8192)[0:1, :]
        s_stage = [bf16(RE, 4096 + i * 256, 512) for i in range(2)]
        lnrow_ring = [f32(RE, 512 * i, 512)[0:1, :] for i in range(2)]
        rstd_ring = [bf16(RE, 1024 + 256 * i, 512)[0:1, :] for i in range(2)]
        s_tile_sb = bf16(RE, 0, 8192)
        t_comb = f32(RE, 4096, 512)

        xt_ring = [bf16(RB, i * 4096, 8192) for i in range(2)]
        sg_bf = bf16(RB, 0, 8192)
        qdg_blk = [bf16(RB, 4096 + i * 1024, 2048) for i in range(2)]
        oraw_blk = [bf16(RB, 6144 + i * 1024, 2048) for i in range(2)]
        contrib_sb = f32(RB, 8192, 2048)
        sprev_f = f32(RB, 10240, 2048)
        ms_rows = bf16(RB, 8192, 8192)[0:1, :]
        xn_e = bf16(RB, 0, 8192)
        xe_ring = [f32(RB, 4096 + i * 512, 512) for i in range(2)]
        oute_ring = [f32(RB, 5120 + i * 512, 512) for i in range(2)]
        g_ring = [bf16(RB, 6144 + i * 256, 512) for i in range(2)]

        xsq_a = bf16(RA, 0, 8192)
        q_rot = bf16(RA, 0, 8192)
        k_rot = bf16(RA, 4096, 8192)
        v_buf = bf16(RA, 8192, 8192)

        o_full_r = [bf16(RA, 4096 * i, 8192) for i in range(2)]
        o_n_bf = bf16(RA, 8192, 8192)
        xsq_d = bf16(RA, 12288, 8192)
        h1_bf = bf16(RA, 0, 32768)

        E = sch.emit

        def dma(dst, src, reads=(), writes=(), stream="dmisc", q="sp"):
            E(q, lambda e, d=dst, s=src: e.dma_start(out=d, in_=s),
              reads=reads, writes=writes, sem=sch.sem(stream))

        pre_keys = ["cc", "ss", "dt", "dkt", "dq", "sel", "cmask", "epsc",
                    "nw", "b2", "b1"]
        dma(cc_sb, cc_t, writes=["cc"], stream="pre")
        dma(ss_sb, ss_t, writes=["ss"], stream="pre")
        dma(dt_sb, dt_t, writes=["dt"], stream="pre")
        dma(dkt_sb, dkt_t, writes=["dkt"], stream="pre")
        dma(dq_sb, dq_t, writes=["dq"], stream="pre")
        dma(sel_col, sel_in, writes=["sel"], stream="pre")
        dma(cmask_col, cmask_in, writes=["cmask"], stream="pre")
        dma(epsc, epsc_in, writes=["epsc"], stream="pre")
        dma(nw_c, nw_col, writes=["nw"], stream="pre")
        dma(b2_c, b2_col, writes=["b2"], stream="pre")
        dma(b1_c, b1_col, writes=["b1"], stream="pre")
        sch.bump_writes_to_total(pre_keys, "pre")
        pre2_keys = ["ones", "ident", "wg1", "wg2", "gw"]
        dma(ones_bf, ones_in, writes=["ones"], stream="pre2", q="pool")
        dma(ident_bf, ident_in, writes=["ident"], stream="pre2", q="pool")
        dma(wg1_bf.rearrange("p (k c) -> p k c", c=128),
            wg1.rearrange("(k p) c -> p k c", p=128), writes=["wg1"], stream="pre2", q="pool")
        dma(wg2_bf, wg2, writes=["wg2"], stream="pre2", q="pool")
        dma(gw_bf.rearrange("p (k c) -> p k c", c=8),
            gate_w_in.rearrange("(k p) c -> p k c", p=128), writes=["gw"], stream="pre2", q="pool")
        sch.bump_writes_to_total(pre2_keys, "pre2")

        for i in range(2):
            dma(wo_bf_d[i * 1024:(i + 1) * 1024, :], wo[i * 1024:(i + 1) * 1024, :],
                writes=["wo_bf"], stream="wc0", q="pool")
        sch.bump_writes_to_total(["wo_bf"], "wc0")

        def emit_p1_cast(lo, hi):
            for i in range(lo, hi):
                gate = ["cc_out"] if i < 2 else ["s_d:15:0"]
                dma(p1t_bf_d[i * 512:(i + 1) * 512, :], p1t[i * 512:(i + 1) * 512, :],
                    reads=gate, writes=["p1_bf"], stream="wc1", q="pool")
            sch.bump_writes_to_total(["p1_bf"], "wc1")

        def emit_p2_cast():
            for oc in range(KC):
                dma(p2t_bf_d[oc * H1:(oc + 1) * H1, :], p2t[oc * H1:(oc + 1) * H1, :],
                    reads=["s_d:15:1"], writes=[f"p2c:{oc}"], stream=f"wc2{oc}", q="pool")

        inv_d = 1.0 / float(D)
        xbf_ring = [bf16(RB, 4096 * i, 8192) for i in range(2)]
        xqv = xsq_a.rearrange("p (k t) -> p k t", t=TT)
        wg1v = wg1_bf.rearrange("p (k c) -> p k c", c=128)
        for tt in range(NT + 1):
            if tt < NT:
                dma(xbf_ring[tt % 2].rearrange("p (k t) -> p k t", t=TT),
                    x_fm.rearrange("(k p) t -> p k t", p=128)[:, :, tt * TT:(tt + 1) * TT],
                    writes=[f"xbf{tt % 2}"], stream=f"xbf{tt % 2}", q="pool")
            if tt == 0:
                continue
            t0 = (tt - 1) * TT
            xbk = f"xbf{(tt - 1) % 2}"
            xbv = xbf_ring[(tt - 1) % 2].rearrange("p (k t) -> p k t", t=TT)
            xbf_flat = xbf_ring[(tt - 1) % 2]
            E("dve", lambda e, x_=xbf_flat: e.tensor_tensor(
                out=xsq_a, in0=x_, in1=x_, op=ALU.mult),
              reads=[xbk], writes=["xsq"])
            for k in range(KC):
                E("pe", lambda e, k=k, xbv=xbv: e.matmul(pb[0][0:1, :], ones_bf[:, 0:1],
                                                         xbv[:, k, :],
                                                         start=(k == 0), stop=(k == KC - 1)),
                  reads=[xbk, "ones"], writes=["ps0"], inc=(k == KC - 1))
            for k in range(KC):
                E("pe", lambda e, k=k: e.matmul(pb[1][0:1, :], ones_bf[:, 0:1], xqv[:, k, :],
                                                start=(k == 0), stop=(k == KC - 1)),
                  reads=["xsq", "ones"], writes=["ps1"], inc=(k == KC - 1))
            E("dve", lambda e: e.tensor_scalar(out=m_row, in0=pb[0][0:1, :], scalar1=inv_d,
                                               scalar2=None, op0=ALU.mult),
              reads=["ps0"], writes=["mrow"])
            E("pool", lambda e: e.tensor_tensor(out=msq_row, in0=m_row, in1=m_row, op=ALU.mult),
              reads=["mrow"], writes=["msq"])
            E("dve", lambda e: e.scalar_tensor_tensor(out=var_row, in0=pb[1][0:1, :], scalar=inv_d,
                                                      in1=msq_row, op0=ALU.mult, op1=ALU.subtract),
              reads=["ps1", "msq"], writes=["var"])
            E("act", lambda e: e.activation(out=ln_row, in_=var_row, func=AF.Ln,
                                            bias=epsc[0:1, 0:1], scale=1.0),
              reads=["var", "epsc"], writes=["lnr"])
            E("act", lambda e: e.activation(out=rstd_f, in_=ln_row, func=AF.Exp, scale=-0.5),
              reads=["lnr"], writes=["rstdf"])
            E("dve", lambda e: e.tensor_copy(out=rstd_bf_row, in_=rstd_f),
              reads=["rstdf"], writes=["rstdb"])
            E("pool", lambda e: e.tensor_tensor(out=nmr_bf_row, in0=m_row,
                                                in1=rstd_f, op=ALU.mult),
              reads=["mrow", "rstdf"], writes=["nmrb"])
            E("pe", lambda e: e.matmul(pb[2][:], ones_bf[0:1, :], rstd_bf_row, start=True, stop=True),
              reads=["rstdb", "ones"], writes=["ps2"])
            E("pe", lambda e: e.matmul(pb[3][:], ones_bf[0:1, :], nmr_bf_row, start=True, stop=True),
              reads=["nmrb", "ones"], writes=["ps3"])
            E("dve", lambda e: e.tensor_copy(out=r2_sb, in_=pb[3][:]), reads=["ps3"], writes=["r2"])
            for kp in range(0, KC, 2):
                for k in (kp, kp + 1):
                    E("dve", lambda e, k=k, xbv=xbv: e.tensor_tensor(
                        out=a_t2r[k % 2], in0=xbv[:, k, :], in1=pb[2][:], op=ALU.mult),
                      reads=[xbk, "ps2"], writes=[f"t2{k % 2}"])
                for k in (kp, kp + 1):
                    E("dve", lambda e, k=k, xbv=xbv: e.tensor_tensor(
                        out=xbv[:, k, :], in0=a_t2r[k % 2], in1=r2_sb, op=ALU.subtract),
                      reads=[f"t2{k % 2}", "r2"], writes=[xbk])
            dma(xn_d.rearrange("(k p) t -> p k t", p=128)[:, :, t0:t0 + TT], xbv,
                reads=[xbk], writes=[f"xn_d:{tt - 1}"], stream="xnst", q="act")
            for k in range(KC):
                E("pe", lambda e, k=k, xbv=xbv: e.matmul(pb[4][:], wg1v[:, k, :], xbv[:, k, :],
                                                         start=(k == 0), stop=(k == KC - 1)),
                  reads=[xbk, "wg1"], writes=["ps4"], inc=(k == KC - 1))
            E("act", lambda e, t0=t0: e.copy(out=g1_bf[:, t0:t0 + TT], in_=pb[4][:]),
              reads=["ps4"], writes=["g1"])
            gwv = gw_bf.rearrange("p (k c) -> p k c", c=8)
            for k in range(KC):
                E("pe", lambda e, k=k, xbv=xbv: e.matmul(pb[5][0:8, :], gwv[:, k, :], xbv[:, k, :],
                                                         start=(k == 0), stop=(k == KC - 1)),
                  reads=[xbk, "gw"], writes=["ps5"], inc=(k == KC - 1))

        sch.phase_barrier()

        qrv = q_rot.rearrange("p (h t) -> p h t", t=TPC)
        krv = k_rot.rearrange("p (h t) -> p h t", t=TPC)
        vbv = v_buf.rearrange("p (c e) -> p c e", e=512)

        def qk_instance(wsv, wslk, xtv, xk, tt, hh, dstv, dkey, bank, emit_qdg_g):
            t0 = tt * TT
            ring = (tt * GH + hh) % 4
            q_s = qs_ring[ring]
            q_w = qw_ring[ring]
            qsk2 = f"q_s{ring}"
            qwk2 = f"q_w{ring}"
            for k in range(KC):
                E("pe", lambda e, k=k, hh=hh, bank=bank, xtv=xtv, wsv=wsv: e.matmul(
                    pb[bank][:], wsv[:, k, hh * 128:(hh + 1) * 128], xtv[:, k, :],
                    start=(k == 0), stop=(k == KC - 1)),
                  reads=[xk, wslk], writes=[f"ps{bank}"], inc=(k == KC - 1))
            E("act", lambda e, bank=bank, q_s=q_s: e.activation(
                out=q_s, in_=pb[bank][:], func=AF.Silu),
              reads=[f"ps{bank}"], writes=[qsk2])
            E("act", lambda e, q_s=q_s, q_w=q_w: e.copy(
                out=q_w[0:64, :], in_=q_s[64:128, :]),
              reads=[qsk2], writes=[qwk2])
            E("act", lambda e, q_s=q_s, q_w=q_w: e.copy(
                out=q_w[64:128, :], in_=q_s[0:64, :]),
              reads=[qsk2], writes=[qwk2])
            E("pool", lambda e, t0=t0, q_w=q_w: e.tensor_tensor(
                out=q_w, in0=q_w, in1=ss_sb[:, t0:t0 + TT], op=ALU.mult),
              reads=[qwk2, "ss"], writes=[qwk2])
            E("dve", lambda e, t0=t0, q_s=q_s: e.tensor_tensor(
                out=q_s, in0=q_s, in1=cc_sb[:, t0:t0 + TT], op=ALU.mult),
              reads=[qsk2, "cc"], writes=[qsk2])
            E("dve", lambda e, hh=hh, t0=t0, dstv=dstv, q_s=q_s, q_w=q_w:
                e.tensor_tensor(out=dstv[:, hh, t0:t0 + TT], in0=q_s, in1=q_w,
                                op=ALU.add),
              reads=[qsk2, qwk2], writes=[dkey])

        for g in range(NG):
            for hh in range(GH):
                E("dve", lambda e, g=g, hh=hh: e.memset(
                    s_all[:, g * 512 + hh * 128:(g * 512 + (hh + 1) * 128)], 0.0),
                  writes=[f"S:{g}:{hh}"])
            E("dve", lambda e: e.memset(sbf_ring[0], 0.0), writes=["sbf0"])
            E("dve", lambda e: e.memset(sbf_ring[1], 0.0), writes=["sbf1"])

            wsl_q = wslab2[0].rearrange("p (k c) -> p k c", c=512)
            wsl_k = wslab2[1].rearrange("p (k c) -> p k c", c=512)
            wsl_v = wsl_q
            dma(wsl_q, wq.rearrange("(k p) c -> p k c", p=128)[:, :, g * 512:(g + 1) * 512],
                writes=["wsl0"], stream="wsl0", q="pool")
            dma(wsl_k, wk.rearrange("(k p) c -> p k c", p=128)[:, :, g * 512:(g + 1) * 512],
                writes=["wsl1"], stream="wsl1", q="pool")

            for tt in range(NT):
                xt = xt_ring[tt % 2]
                xk = f"xt:{tt % 2}"
                xtv = xt.rearrange("p (k t) -> p k t", t=TT)
                dma(xtv, xn_d.rearrange("(k p) t -> p k t", p=128)[:, :, tt * TT:(tt + 1) * TT],
                    reads=[f"xn_d:{tt}"], writes=[xk], stream=f"xt{tt % 2}")
                for hh in range(GH):
                    qk_instance(wsl_q, "wsl0", xtv, xk, tt, hh, qrv, "q_rot",
                                (tt * GH + hh) % 4, g)
            dma(wsl_v, wv.rearrange("(k p) c -> p k c", p=128)[:, :, g * 512:(g + 1) * 512],
                writes=["wsl0"], stream="wsl0", q="pool")

            def emit_sT_kT(c):
                psA = pb[2 + c % 2]
                psB = pbf[c % 2]
                kA = f"ps{2 + c % 2}"
                kB = f"psb{c % 2}"
                for hh in range(GH):
                    E("pe", lambda e, hh=hh, c=c, psA=psA: e.matmul(
                        psA[:, hh * 128:(hh + 1) * 128],
                        krv[:, hh, c * 128:(c + 1) * 128],
                        qrv[:, hh, c * 128:(c + 1) * 128],
                        start=True, stop=True),
                      reads=["k_rot", "q_rot"], writes=[kA], inc=(hh == GH - 1))
                for hh in range(GH):
                    E("pe", lambda e, hh=hh, c=c, psB=psB: e.transpose(
                        psB[:, hh * 128:(hh + 1) * 128],
                        krv[:, hh, c * 128:(c + 1) * 128], ident_bf),
                      reads=["k_rot", "ident"], writes=[kB], inc=(hh == GH - 1))

            for tt in range(NT):
                xt = xt_ring[tt % 2]
                xk = f"xt:{tt % 2}"
                xtv = xt.rearrange("p (k t) -> p k t", t=TT)
                dma(xtv, xn_d.rearrange("(k p) t -> p k t", p=128)[:, :, tt * TT:(tt + 1) * TT],
                    reads=[f"xn_d:{tt}"], writes=[xk], stream=f"xt{tt % 2}")
                for hh in range(GH):
                    qk_instance(wsl_k, "wsl1", xtv, xk, tt, hh, krv, "k_rot",
                                (tt * GH + hh) % 2, g)
                for tc in range(4):
                    tcg = tt * 4 + tc
                    bank = (tc % 2)
                    for k in range(KC):
                        E("pe", lambda e, k=k, tc=tc, bank=bank, xtv=xtv: e.matmul(
                            pb[bank][:], xtv[:, k, tc * 128:(tc + 1) * 128], wsl_v[:, k, :],
                            start=(k == 0), stop=(k == KC - 1)),
                          reads=[xk, "wsl0"], writes=[f"ps{bank}"], inc=(k == KC - 1))
                    E("act", lambda e, tcg=tcg, bank=bank: e.copy(out=vbv[:, tcg, :],
                                                                  in_=pb[bank][:]),
                      reads=[f"ps{bank}"], writes=["v_buf"])
                emit_sT_kT(tt * 4)
                for c in range(tt * 4, tt * 4 + 4):
                    psA = pb[2 + c % 2]
                    psC = pb[4 + c % 2]
                    psD = pb[2 + c % 2]
                    psB = pbf[c % 2]
                    kA = f"ps{2 + c % 2}"
                    kCk = f"ps{4 + c % 2}"
                    kD = f"ps{2 + c % 2}"
                    kB = f"psb{c % 2}"
                    sd_bf = sd_ring[c % 2]
                    kd_bf = kd_ring[c % 2]
                    qd_bf = qd_ring[c % 2]
                    sdk, kdk, qdk = f"sd{c % 2}", f"kd{c % 2}", f"qd{c % 2}"
                    sdv = sd_bf.rearrange("p (h t) -> p h t", t=128)
                    qdv = qd_bf.rearrange("p (h t) -> p h t", t=128)
                    kdv = kd_bf.rearrange("p (h e) -> p h e", e=128)
                    sbf_in = sbf_ring[c % 2]
                    sbf_out = sbf_ring[(c + 1) % 2]
                    sbk_in, sbk_out = f"sbf{c % 2}", f"sbf{(c + 1) % 2}"
                    sbv = sbf_in.rearrange("p (h e) -> p h e", e=128)
                    E("dve", lambda e, g=g, psA=psA, sd_bf=sd_bf: e.tensor_tensor(
                        out=sd_bf, in0=psA[:], in1=dt_sb[:, g * 512:(g + 1) * 512],
                        op=ALU.mult),
                      reads=[kA, "dt"], writes=[sdk])
                    E("dve", lambda e, g=g, psB=psB, kd_bf=kd_bf: e.tensor_tensor(
                        out=kd_bf, in0=psB[:], in1=dkt_sb[:, g * 512:(g + 1) * 512],
                        op=ALU.mult),
                      reads=[kB, "dkt"], writes=[kdk])
                    E("pool", lambda e, g=g, c=c, qdv=qdv: e.tensor_tensor(
                        out=qdv[:, :, :],
                        in0=qrv[:, :, c * 128:(c + 1) * 128],
                        in1=dq_sb[:, g * 512:(g + 1) * 512].rearrange(
                            "p (h t) -> p h t", t=128),
                        op=ALU.mult),
                      reads=["q_rot", "dq"], writes=[qdk])
                    if c + 1 <= tt * 4 + 3:
                        emit_sT_kT(c + 1)
                    for hh in range(GH):
                        E("pe", lambda e, hh=hh, c=c, psD=psD, kdv=kdv: e.matmul(
                            psD[:, hh * 128:(hh + 1) * 128],
                            kdv[:, hh, :], vbv[:, c, hh * 128:(hh + 1) * 128],
                            start=True, stop=True),
                          reads=[kdk, "v_buf"], writes=[kD], inc=(hh == GH - 1))
                    for hh in range(GH):
                        E("pe", lambda e, hh=hh, c=c, psC=psC, sdv=sdv: e.matmul(
                            psC[:, hh * 128:(hh + 1) * 128],
                            vbv[:, c, hh * 128:(hh + 1) * 128],
                            sdv[:, hh, :],
                            start=True, stop=False),
                          reads=["v_buf", sdk], writes=[kCk], inc=False)
                        E("pe", lambda e, hh=hh, psC=psC, qdv=qdv, sbv=sbv: e.matmul(
                            psC[:, hh * 128:(hh + 1) * 128],
                            sbv[:, hh, :], qdv[:, hh, :],
                            start=False, stop=True),
                          reads=[sbk_in, qdk], writes=[kCk], inc=(hh == GH - 1))
                    for hh in range(GH):
                        h = g * GH + hh
                        E("dve", lambda e, hh=hh, h=h, g=g, psD=psD: e.scalar_tensor_tensor(
                            out=s_all[:, g * 512 + hh * 128: g * 512 + (hh + 1) * 128],
                            in0=s_all[:, g * 512 + hh * 128: g * 512 + (hh + 1) * 128],
                            scalar=lam_c[h],
                            in1=psD[:, hh * 128:(hh + 1) * 128],
                            op0=ALU.mult, op1=ALU.add),
                          reads=[kD, f"S:{g}:{hh}"], writes=[f"S:{g}:{hh}"])
                    E("act", lambda e, g=g, sbf_out=sbf_out: e.copy(
                        out=sbf_out, in_=s_all[:, g * 512:(g + 1) * 512]),
                      reads=[f"S:{g}:0", f"S:{g}:1", f"S:{g}:2", f"S:{g}:3"],
                      writes=[sbk_out])
                    qst = qdg_stage[c % 2]
                    qsk = f"qdgs:{c % 2}"
                    for hh in range(GH):
                        h = g * GH + hh
                        lam_pw = float(np.float64(lam_c[h]) ** c)
                        E("dve", lambda e, hh=hh, lam_pw=lam_pw, qst=qst, qdv=qdv:
                            e.tensor_scalar(
                                out=qst.rearrange("p (h t) -> p h t", t=128)[:, hh, :],
                                in0=qdv[:, hh, :], scalar1=lam_pw, scalar2=None,
                                op0=ALU.mult),
                          reads=[qdk], writes=[qsk])
                    dma(qdg_d.rearrange("(a p) (b t) -> p a b t", p=128, t=128)
                        [:, g * NCH + c, :, :],
                        qst.rearrange("p (h t) -> p h t", t=128),
                        reads=[qsk], writes=[f"qdg_d:{g}:{c}"], stream=f"qdgst{c % 2}",
                        q="act")
                    ost = oraw_stage[c % 2]
                    osk = f"oraws:{c % 2}"
                    E("act", lambda e, psC=psC, ost=ost: e.copy(out=ost, in_=psC[:]),
                      reads=[kCk], writes=[osk])
                    dma(oraw_d[(g * NCH + c) * HD:(g * NCH + c + 1) * HD, :], ost,
                        reads=[osk], writes=[f"oraw_d:{g}:{c}"], stream=f"orawst{c % 2}",
                        q="act")

        E("dve", lambda e: e.tensor_scalar(out=contrib_sb, in0=s_all, scalar1=cmask_col,
                                           scalar2=None, op0=ALU.mult),
          reads=[f"S:{g_}:{h_}" for g_ in range(4) for h_ in range(4)] + ["cmask"],
          writes=["contrib"])
        dma(cc_in_d.rearrange("(h p) e -> p h e", p=128),
            contrib_sb.rearrange("p (h e) -> p h e", e=128),
            reads=["contrib"], writes=["cc_in"], stream="ccin", q="act")
        E("pool", lambda e: e.collective_compute(
            "AllReduce", ALU.add,
            replica_groups=[[0, 1], [2, 3], [4, 5], [6, 7]],
            ins=[cc_in_d], outs=[cc_out_d]),
          reads=["cc_in"], writes=["cc_out"], sem="coll")
        emit_p1_cast(0, 4)
        dma(sprev_f.rearrange("p (h e) -> p h e", e=128),
            cc_out_d.rearrange("(h p) e -> p h e", p=128),
            reads=["cc_out"], writes=["sprevf", "dkt", "dq"], stream="sprevf")
        E("dve", lambda e: e.tensor_scalar(out=s_prev_bf, in0=sprev_f, scalar1=sel_col,
                                           scalar2=None, op0=ALU.mult),
          reads=["sprevf", "sel"], writes=["sprev"])
        emit_p2_cast()

        onv = o_n_bf.rearrange("p (h t) -> p h t", t=TT)
        sgv = sg_bf.rearrange("p (h t) -> p h t", t=TT)
        xqvd = xsq_d.rearrange("p (h t) -> p h t", t=TT)
        spv = s_prev_bf.rearrange("p (h e) -> p h e", e=128)
        orv = oraw_d.rearrange("(a p) (b t) -> p a b t", p=128, t=128)
        qrv_d = qdg_d.rearrange("(a p) (b t) -> p a b t", p=128, t=128)
        for tt in range(NT):
            t0 = tt * TT
            o_full = o_full_r[tt % 2]
            ofk = f"of{tt % 2}"
            ofv = o_full.rearrange("p (h t) -> p h t", t=TT)
            for g in range(NG):
                qblk = qdg_blk[g % 2]
                qbk = f"qdgb:{g % 2}"
                oblk = oraw_blk[g % 2]
                obk = f"orawb:{g % 2}"
                rblk = [f"qdg_d:{g}:{tt * 4 + i}" for i in range(4)]
                rblk2 = [f"oraw_d:{g}:{tt * 4 + i}" for i in range(4)]
                dma(qblk.rearrange("p (c w) -> p c w", w=512),
                    qdg_d.rearrange("(a p) w -> p a w", p=128)
                    [:, g * NCH + tt * 4: g * NCH + tt * 4 + 4, :],
                    reads=rblk, writes=[qbk], stream=f"qdgl{g % 2}")
                dma(oblk.rearrange("p (c w) -> p c w", w=512),
                    oraw_d.rearrange("(a p) w -> p a w", p=128)
                    [:, g * NCH + tt * 4: g * NCH + tt * 4 + 4, :],
                    reads=rblk2, writes=[obk], stream=f"orawl{g % 2}", q="act")
                qbv = qblk.rearrange("p (c hh t) -> p c hh t", hh=GH, t=128)
                obv = oblk.rearrange("p (c hh t) -> p c hh t", hh=GH, t=128)
                for hh in range(GH):
                    h = g * GH + hh
                    bank = 2 + h % 2
                    E("pe", lambda e, h=h, hh=hh, qbv=qbv, bank=bank: e.matmul(
                        pb[bank][:], spv[:, h, :], qbv[:, :, hh, :],
                        start=True, stop=True),
                      reads=["sprev", qbk], writes=[f"ps{bank}"])
                    E("dve", lambda e, h=h, hh=hh, obv=obv, bank=bank, ofv=ofv:
                        e.scalar_tensor_tensor(
                            out=ofv[:, h, :], in0=obv[:, :, hh, :], scalar=1.0,
                            in1=pb[bank][:].rearrange("p (c t) -> p c t", t=128),
                            op0=ALU.mult, op1=ALU.add),
                      reads=[obk, f"ps{bank}"], writes=[ofk])
            for h in range(HEADS):
                gbank = 4 + h % 2
                E("pe", lambda e, h=h, t0=t0, gbank=gbank: e.matmul(
                    pb[gbank][:], wg2_bf[:, h * 128:(h + 1) * 128], g1_bf[:, t0:t0 + TT],
                    start=True, stop=True),
                  reads=["wg2", "g1"], writes=[f"ps{gbank}"])
                E("act", lambda e, h=h, gbank=gbank: e.activation(
                    out=sgv[:, h, :], in_=pb[gbank][:], func=AF.Sigmoid),
                  reads=[f"ps{gbank}"], writes=["sg"])
            E("dve", lambda e, o_full=o_full: e.tensor_tensor(
                out=o_full, in0=o_full, in1=sg_bf, op=ALU.mult),
              reads=[ofk, "sg"], writes=[ofk])
            E("act", lambda e, o_full=o_full: e.activation(
                out=xsq_d, in_=o_full, func=AF.Square),
              reads=[ofk], writes=["xsqd"])
            for h in range(HEADS):
                mbank = 4 + h % 2
                E("pe", lambda e, h=h, mbank=mbank: e.matmul(
                    pb[mbank][0:1, :], ones_bf[:, 0:1], xqvd[:, h, :], start=True, stop=True),
                  reads=["xsqd", "ones"], writes=[f"ps{mbank}"])
                E("act", lambda e, h=h, mbank=mbank: e.copy(
                    out=ms_rows[:, h * 512:(h + 1) * 512], in_=pb[mbank][0:1, :]),
                  reads=[f"ps{mbank}"], writes=["msrow"])
            for half in range(2):
                E("act", lambda e, half=half: e.activation(
                    out=rstd_rows[:, half * 4096:(half + 1) * 4096],
                    in_=ms_rows[:, half * 4096:(half + 1) * 4096], func=AF.Ln,
                    bias=epsc[0:1, 1:2], scale=1.0 / HD),
                  reads=["msrow", "epsc"], writes=["rstdrow"])
            for half in range(2):
                E("act", lambda e, half=half: e.activation(
                    out=rstd_rows[:, half * 4096:(half + 1) * 4096],
                    in_=rstd_rows[:, half * 4096:(half + 1) * 4096], func=AF.Exp, scale=-0.5),
                  reads=["rstdrow"], writes=["rstdrow"])
            for h in range(HEADS):
                mbank = 4 + h % 2
                E("pe", lambda e, h=h, mbank=mbank: e.matmul(
                    pb[mbank][:], ones_bf[0:1, :], rstd_rows[:, h * 512:(h + 1) * 512],
                    start=True, stop=True),
                  reads=["rstdrow", "ones", f"ps{mbank}"], writes=[f"ps{mbank}"])
                E("dve", lambda e, h=h, mbank=mbank, ofv=ofv: e.scalar_tensor_tensor(
                    out=onv[:, h, :], in0=ofv[:, h, :], scalar=nw_c[:, h:h + 1],
                    in1=pb[mbank][:], op0=ALU.mult, op1=ALU.mult),
                  reads=[ofk, "nw", f"ps{mbank}"], writes=["o_n"])
            for oq in range(4):
                wsl = wo_slab[oq % 2]
                wslk = f"wosl:{oq % 2}"
                dma(wsl.rearrange("p (k c) -> p k c", c=512),
                    wo_bf_d.rearrange("(k p) c -> p k c", p=128)[:, :, oq * 512:(oq + 1) * 512],
                    reads=["wo_bf"], writes=[wslk], stream=f"wosl{oq % 2}")
                wslv = wsl.rearrange("p (k c) -> p k c", c=512)
                for oc in range(4):
                    bank = oc % 2
                    for k in range(KC):
                        E("pe", lambda e, k=k, oc=oc, bank=bank, wslv=wslv: e.matmul(
                            pb[bank][:], wslv[:, k, oc * 128:(oc + 1) * 128], onv[:, k, :],
                            start=(k == 0), stop=(k == KC - 1)),
                          reads=[wslk, "o_n"], writes=[f"ps{bank}"], inc=(k == KC - 1))
                    sst = s_stage[oc % 2]
                    sstk = f"sstg:{oc % 2}"
                    E("act", lambda e, bank=bank, sst=sst: e.copy(out=sst, in_=pb[bank][:]),
                      reads=[f"ps{bank}"], writes=[sstk])
                    o = oq * 4 + oc
                    dma(s_d.rearrange("(o p) t -> p o t", p=128)[:, o, t0:t0 + TT], sst,
                        reads=[sstk], writes=[f"s_d:{o}:{tt}"], stream=f"sst{oc % 2}", q="act")

        sch.phase_barrier()

        h1v = h1_bf.rearrange("p (m t) -> p m t", t=TT)
        xnev = xn_e.rearrange("p (k t) -> p k t", t=TT)
        stv = s_tile_sb.rearrange("p (o t) -> p o t", t=TT)
        for tt in range(NT):
            t0 = tt * TT
            dma(xnev, xn_d.rearrange("(k p) t -> p k t", p=128)[:, :, t0:t0 + TT],
                reads=[f"xn_d:{tt}"], writes=["xn_e"], stream="xne")
            dma(stv, s_d.rearrange("(o p) t -> p o t", p=128)[:, :, t0:t0 + TT],
                reads=[f"s_d:{o_}:{tt}" for o_ in range(KC)], writes=["s_e"], stream="se")
            for ms in range(32):
                sl = p1_slab[ms % 2]
                slk = f"p1s:{ms % 2}"
                dma(sl.rearrange("p (k c) -> p k c", c=256),
                    p1t_bf_d.rearrange("(k p) c -> p k c", p=128)[:, :, ms * 256:(ms + 1) * 256],
                    reads=["p1_bf"], writes=[slk], stream=f"p1s{ms % 2}")
                slv = sl.rearrange("p (k c) -> p k c", c=256)
                for mi in range(2):
                    m = ms * 2 + mi
                    bank = m % 4
                    for k in range(KC):
                        E("pe", lambda e, k=k, mi=mi, bank=bank, slv=slv: e.matmul(
                            pb[bank][:], slv[:, k, mi * 128:(mi + 1) * 128], xnev[:, k, :],
                            start=(k == 0), stop=(k == KC - 1)),
                          reads=[slk, "xn_e"], writes=[f"ps{bank}"], inc=(k == KC - 1))
                    E("act", lambda e, m=m, bank=bank: e.activation(
                        out=h1v[:, m, :], in_=pb[bank][:], func=AF.Gelu,
                        bias=b1_c[:, m:m + 1], scale=1.0),
                      reads=[f"ps{bank}", "b1"], writes=["h1"])
            for oc in range(KC):
                sl = p2_slab[oc % 2]
                slk = f"p2s:{oc % 2}"
                dma(sl.rearrange("p (m c) -> p m c", c=128),
                    p2t_bf_d[oc * H1:(oc + 1) * H1, :].rearrange("(m p) c -> p m c", p=128),
                    reads=[f"p2c:{oc}"], writes=[slk], stream=f"p2s{oc % 2}")
                slv = sl.rearrange("p (m c) -> p m c", c=128)
                bank = 4 + oc % 2
                for m in range(MC):
                    E("pe", lambda e, m=m, bank=bank, slv=slv: e.matmul(
                        pb[bank][:], slv[:, m, :], h1v[:, m, :],
                        start=(m == 0), stop=(m == MC - 1)),
                      reads=[slk, "h1"], writes=[f"ps{bank}"], inc=(m == MC - 1))
                gr = g_ring[oc % 2]
                grk = f"g_e:{oc % 2}"
                E("act", lambda e, oc=oc, bank=bank, gr=gr: e.activation(
                    out=gr, in_=pb[bank][:], func=AF.Silu, bias=b2_c[:, oc:oc + 1], scale=1.0),
                  reads=[f"ps{bank}", "b2"], writes=[grk])
                xr = xe_ring[oc % 2]
                xrk = f"x_e:{oc % 2}"
                dma(xr, x_fm[oc * 128:(oc + 1) * 128, t0:t0 + TT], writes=[xrk],
                    stream=f"xe{oc % 2}")
                E("dve", lambda e, oc=oc, gr=gr: e.tensor_tensor(
                    out=t_comb, in0=gr, in1=stv[:, oc, :], op=ALU.mult),
                  reads=[grk, "s_e"], writes=["t_comb"])
                our = oute_ring[oc % 2]
                ourk = f"out_e:{oc % 2}"
                E("dve", lambda e, xr=xr, our=our: e.tensor_tensor(
                    out=our, in0=t_comb, in1=xr, op=ALU.add),
                  reads=["t_comb", xrk], writes=[ourk])
                dma(y[oc * 128:(oc + 1) * 128, t0:t0 + TT], our,
                    reads=[ourk], writes=["y"], stream=f"ystr{oc % 2}", q="act")

        if debug:
            dxn = nc.dram_tensor("dxn", [D, TPC], BF16, kind="ExternalOutput").ap()
            dqdg = nc.dram_tensor("dqdg", [NG * NCH * HD, GH * CH], BF16, kind="ExternalOutput").ap()
            doraw = nc.dram_tensor("doraw", [NG * NCH * HD, GH * CH], BF16,
                                   kind="ExternalOutput").ap()
            dsd = nc.dram_tensor("dsd", [D, TPC], BF16, kind="ExternalOutput").ap()
            dg1 = nc.dram_tensor("dg1", [HD, TPC], BF16, kind="ExternalOutput").ap()
            dsprev = nc.dram_tensor("dsprev", [HD, HEADS * HD], BF16,
                                    kind="ExternalOutput").ap()
            dsall = nc.dram_tensor("dsall", [HD, HEADS * HD], F32,
                                   kind="ExternalOutput").ap()
            sch.phase_barrier()
            dma(dxn, xn_d, writes=["dbg"], stream="dbg")
            dma(dqdg, qdg_d, writes=["dbg"], stream="dbg")
            dma(doraw, oraw_d, writes=["dbg"], stream="dbg")
            dma(dsd, s_d, writes=["dbg"], stream="dbg")
            dma(dg1, g1_bf, reads=["g1"], writes=["dbg"], stream="dbg")
            dma(dsprev, s_prev_bf, reads=["sprev"], writes=["dbg"], stream="dbg")
            dma(dsall, s_all, reads=[f"S:{g_}:{h_}" for g_ in range(4) for h_ in range(4)],
                writes=["dbg"], stream="dbg")

        for t in list(sch.cnt):
            if t in ("pe", "act", "dve", "pool"):
                continue
            if sch.cnt[t] > 0:
                sch.barrier_wait("sp", t, sch.cnt[t])

        for name in sch.cnt:
            if name not in sems:
                sems[name] = ctx.enter_context(nc.semaphore(f"sem_{name}"))

        def run_queue(engine, q):
            for item in q:
                if item[0] == "wait":
                    engine.wait_ge(sems[item[1]], item[2])
                else:
                    _, fn, semname, step = item
                    inst = fn(engine)
                    if semname is not None:
                        inst.then_inc(sems[semname], step)

        @blk.tensor
        def _(e):
            run_queue(e, sch.queues["pe"])

        @blk.scalar
        def _(e):
            run_queue(e, sch.queues["act"])

        @blk.vector
        def _(e):
            run_queue(e, sch.queues["dve"])

        @blk.gpsimd
        def _(e):
            run_queue(e, sch.queues["pool"])

        @blk.sync
        def _(e):
            run_queue(e, sch.queues["sp"])

    return nc


_CACHE = {}


def kernel(debug=False, **inputs):
    x = np.asarray(inputs["L0_attn_x"], np.float32)
    wq = np.asarray(inputs["L0_attn_Wq"], np.float32)
    wk = np.asarray(inputs["L0_attn_Wk"], np.float32)
    wv = np.asarray(inputs["L0_attn_Wv"], np.float32)
    wo = np.asarray(inputs["L0_attn_Wo"], np.float32)
    wg1 = np.asarray(inputs["L0_attn_Wg1"], np.float32)
    wg2 = np.asarray(inputs["L0_attn_Wg2"], np.float32)
    nw = np.asarray(inputs["L0_attn_norm_weight"], np.float32)
    p1 = np.asarray(inputs["L0_ffn_proj1"], np.float32)
    b1 = np.asarray(inputs["L0_ffn_proj1_bias"], np.float32)
    p2 = np.asarray(inputs["L0_ffn_proj2"], np.float32)
    b2 = np.asarray(inputs["L0_ffn_proj2_bias"], np.float32)
    gate_w = np.asarray(inputs["L0_ffn_gate_w"], np.float32)

    cc, ss, dt, dkt, dq, tabg, _ = _tables()
    p1t = np.ascontiguousarray(p1.T)
    p2t = np.ascontiguousarray(
        p2.T.reshape(H1, KC, HD).transpose(1, 0, 2).reshape(KC * H1, HD))
    nw_col = np.ascontiguousarray(nw.reshape(HEADS, HD).T)
    b1_col = np.ascontiguousarray(b1.reshape(MC, HD).T)
    b2_col = np.ascontiguousarray(b2.reshape(KC, HD).T)
    ones_f = np.ones((HD, HD), np.float32)
    ident_f = np.eye(HD, dtype=np.float32)
    epsc = np.tile(np.array([EPS_LN, EPS_RMS], np.float32)[None, :], (HD, 1))

    ck = f"nc{int(debug)}"
    if ck not in _CACHE:
        _CACHE[ck] = build_program(debug=debug)
    nc = _CACHE[ck]

    in_maps = []
    for c in range(8):
        b, half = c // 2, c % 2
        x_fm = np.ascontiguousarray(x[b, half * TPC:(half + 1) * TPC, :].T)
        sel = np.full((HD, 1), 1.0 if half == 1 else 0.0, np.float32)
        cmask = np.full((HD, 1), 1.0 if half == 0 else 0.0, np.float32)
        in_maps.append({
            "x_fm": x_fm, "wq": wq, "wk": wk, "wv": wv, "wo": wo,
            "wg1": wg1, "wg2": wg2, "nw_col": nw_col,
            "p1t": p1t, "b1_col": b1_col, "p2t": p2t, "b2_col": b2_col,
            "cc_t": cc[half], "ss_t": ss[half],
            "dt_t": dt, "dkt_t": dkt, "dq_t": dq,
            "ones_in": ones_f, "ident_in": ident_f,
            "sel_in": sel, "cmask_in": cmask, "epsc_in": epsc,
            "gate_w_in": gate_w,
        })

    res = run_bass_kernel_spmd(nc, in_maps, core_ids=list(range(8)))

    out = np.empty((B, SEQ, D), np.float32)
    for c in range(8):
        b, half = c // 2, c % 2
        out[b, half * TPC:(half + 1) * TPC, :] = res.results[c]["y"].T
    if debug:
        return out, res.results
    return out


# version 39
# speedup vs baseline: 1.0647x; 1.0647x over previous
import contextlib
import numpy as np
import ml_dtypes
import concourse.bass as bass
import concourse.mybir as mybir
from concourse.bass_utils import run_bass_kernel_spmd

F32 = mybir.dt.float32
BF16 = mybir.dt.bfloat16
AF = mybir.ActivationFunctionType
ALU = mybir.AluOpType

BF = ml_dtypes.bfloat16

B, SEQ, D = 4, 4096, 2048
HEADS, HD, CH = 16, 128, 128
TPC = 2048
NT, TT = 4, 512
KC = 16
NG, GH = 4, 4
NCH = 16
H1, MC = 8192, 64
EPS_LN, EPS_RMS = 1e-5, 1e-6
RELAX_SAME_ENGINE_WAR = False


class Sched:
    ENGS = ("pe", "act", "dve", "pool", "sp")

    def __init__(self):
        self.step = {"pe": 1, "act": 1, "dve": 1, "pool": 1, "coll": 1}
        self.cnt = {e: 0 for e in self.step}
        self.queues = {e: [] for e in Sched.ENGS}
        self.res = {}
        self.wm = {}
        self.pend = {e: [] for e in self.step}

    def sem(self, name, step=16):
        if name not in self.step:
            self.step[name] = step
            self.cnt[name] = 0
            self.pend[name] = []
        return name

    def _r(self, k):
        if k not in self.res:
            self.res[k] = {"w": None, "r": {}}
        return self.res[k]

    def emit(self, eng, fn, reads=(), writes=(), inc=True, sem="same"):
        need = {}

        def _need(t, c, war=False):
            if t == "pe" and eng == "pe":
                return
            if war and t == eng and RELAX_SAME_ENGINE_WAR:
                return
            if need.get(t, 0) < c:
                need[t] = c

        for k in reads:
            r = self.res.get(k)
            if r and r["w"]:
                _need(*r["w"])
        for k in writes:
            r = self.res.get(k)
            if r:
                if r["w"]:
                    _need(*r["w"], war=True)
                for t, c in r["r"].items():
                    _need(t, c, war=True)
        for t, c in sorted(need.items()):
            if self.wm.get((eng, t), 0) >= c:
                continue
            self.queues[eng].append(("wait", t, c))
            self.wm[(eng, t)] = c
        semname = eng if sem == "same" else sem
        if inc:
            self.cnt[semname] += self.step[semname]
            cafter = self.cnt[semname]
            self.queues[eng].append(("inst", fn, semname, self.step[semname]))
            for k, isw in self.pend[semname]:
                r = self._r(k)
                if isw:
                    r["w"] = (semname, cafter)
                    r["r"] = {}
                else:
                    if r["r"].get(semname, 0) < cafter:
                        r["r"][semname] = cafter
            self.pend[semname] = []
            for k in writes:
                r = self._r(k)
                r["w"] = (semname, cafter)
                r["r"] = {}
            for k in reads:
                r = self._r(k)
                if r["r"].get(semname, 0) < cafter:
                    r["r"][semname] = cafter
        else:
            self.queues[eng].append(("inst", fn, None, 0))
            for k in writes:
                self.pend[semname].append((k, True))
            for k in reads:
                self.pend[semname].append((k, False))

    WCAST_SEMS = tuple(["wc0", "wc1"] + [f"wc2{i}" for i in range(16)])

    def phase_barrier(self, exclude=WCAST_SEMS):
        for e in Sched.ENGS:
            for t in list(self.cnt):
                c = self.cnt[t]
                if c == 0 or t == e or t in exclude:
                    continue
                if self.wm.get((e, t), 0) >= c:
                    continue
                self.queues[e].append(("wait", t, c))
                self.wm[(e, t)] = c

    def barrier_wait(self, eng, tgt, cnt):
        if cnt <= 0 or self.wm.get((eng, tgt), 0) >= cnt:
            return
        self.queues[eng].append(("wait", tgt, cnt))
        self.wm[(eng, tgt)] = cnt

    def bump_writes_to_total(self, keys, sem):
        for k in keys:
            r = self.res.get(k)
            if r and r["w"] and r["w"][0] == sem:
                r["w"] = (sem, self.cnt[sem])


def _tables():
    j = np.arange(HD // 2, dtype=np.float32)
    theta = (10000.0 ** (-2.0 * j / HD)).astype(np.float32)
    lam = -(2.0 ** (-8.0 * (np.arange(HEADS, dtype=np.float32) + 1.0) / HEADS))

    cc, ss = [], []
    for half in range(2):
        pos = np.arange(half * TPC, (half + 1) * TPC, dtype=np.float32)
        ang = pos[None, :] * theta[:, None]
        c = np.cos(ang).astype(np.float32)
        s = np.sin(ang).astype(np.float32)
        cc.append(np.ascontiguousarray(np.concatenate([c, c], axis=0)))
        ss.append(np.ascontiguousarray(np.concatenate([-s, s], axis=0)))

    i = np.arange(CH, dtype=np.float32)
    diff = i[None, :] - i[:, None]
    dt = np.zeros((CH, HEADS * CH), np.float32)
    dkt = np.zeros((CH, HEADS * CH), np.float32)
    dq = np.zeros((CH, HEADS * CH), np.float32)
    for h in range(HEADS):
        m = np.where(diff >= 0, np.exp(lam[h] * np.maximum(diff, 0.0)), 0.0)
        dt[:, h * CH:(h + 1) * CH] = m
        dkt[:, h * CH:(h + 1) * CH] = np.exp(lam[h] * (CH - 1.0 - i))[:, None]
        dq[:, h * CH:(h + 1) * CH] = np.exp(lam[h] * (i + 1.0))[None, :]
    p = np.arange(TPC, dtype=np.float32)
    tabg = np.zeros((HEADS * HD, TPC), np.float32)
    for h in range(HEADS):
        tabg[h * HD:(h + 1) * HD, :] = np.exp(lam[h] * (p + 1.0))[None, :]
    lam_c = [float(np.exp(np.float64(lam[h]) * CH)) for h in range(HEADS)]
    return cc, ss, dt.astype(BF), dkt.astype(BF), dq.astype(BF), tabg.astype(BF), lam_c


def build_program(debug=False):
    nc = bass.Bass("TRN2", target_bir_lowering=False)

    def din(name, shape, dt=F32):
        return nc.dram_tensor(name, shape, dt, kind="ExternalInput").ap()

    x_fm = din("x_fm", [D, TPC])
    wq = din("wq", [D, D]); wk = din("wk", [D, D]); wv = din("wv", [D, D])
    wo = din("wo", [D, D])
    wg1 = din("wg1", [D, HD]); wg2 = din("wg2", [HD, D])
    nw_col = din("nw_col", [HD, HEADS])
    p1t = din("p1t", [D, H1]); b1_col = din("b1_col", [HD, MC])
    p2t = din("p2t", [KC * H1, HD]); b2_col = din("b2_col", [HD, KC])
    cc_t = din("cc_t", [HD, TPC]); ss_t = din("ss_t", [HD, TPC])
    dt_t = din("dt_t", [CH, HEADS * CH], BF16)
    dkt_t = din("dkt_t", [CH, HEADS * CH], BF16)
    dq_t = din("dq_t", [CH, HEADS * CH], BF16)
    ones_in = din("ones_in", [HD, HD])
    ident_in = din("ident_in", [HD, HD])
    sel_in = din("sel_in", [HD, 1])
    cmask_in = din("cmask_in", [HD, 1])
    epsc_in = din("epsc_in", [HD, 2])
    gate_w_in = din("gate_w_in", [D, 8])

    y = nc.dram_tensor("y", [D, TPC], F32, kind="ExternalOutput").ap()

    xn_d = nc.dram_tensor("xn_d", [D, TPC], BF16, kind="Internal").ap()
    qdg_d = nc.dram_tensor("qdg_d", [NG * NCH * HD, GH * CH], BF16, kind="Internal").ap()
    oraw_d = nc.dram_tensor("oraw_d", [NG * NCH * HD, GH * CH], BF16, kind="Internal").ap()
    s_d = nc.dram_tensor("s_d", [D, TPC], BF16, kind="Internal").ap()
    wo_bf_d = nc.dram_tensor("wo_bf_d", [D, D], BF16, kind="Internal").ap()
    p1t_bf_d = nc.dram_tensor("p1t_bf_d", [D, H1], BF16, kind="Internal").ap()
    p2t_bf_d = nc.dram_tensor("p2t_bf_d", [KC * H1, HD], BF16, kind="Internal").ap()
    cc_in_d = nc.dram_tensor("cc_in_d", [HEADS * HD, HD], F32, kind="Internal").ap()
    cc_out_d = nc.dram_tensor("cc_out_d", [HEADS * HD, HD], F32, kind="Internal").ap()

    lam_c = _tables()[-1]

    sch = Sched()

    with contextlib.ExitStack() as ctx:
        RA = ctx.enter_context(nc.sbuf_tensor("RA", [128, 16384], F32))
        RB = ctx.enter_context(nc.sbuf_tensor("RB", [128, 12288], F32))
        RC = ctx.enter_context(nc.sbuf_tensor("RC", [128, 12288], F32))
        RD = ctx.enter_context(nc.sbuf_tensor("RD", [128, 6144], F32))
        RE = ctx.enter_context(nc.sbuf_tensor("RE", [128, 5888], F32))

        def f32(reg, off, n):
            return reg[:, off:off + n]

        def bf16(reg, off_f32, n_bf):
            return reg[:, off_f32:off_f32 + n_bf // 2].bitcast(BF16)

        pb = [ctx.enter_context(nc.psum_tensor(f"pb{i}", [128, 512], F32)) for i in range(6)]
        pbf = [ctx.enter_context(nc.psum_tensor(f"pbf{i}", [128, 512], BF16)) for i in range(2)]

        sems = {}
        for e in ("pe", "act", "dve", "pool", "coll"):
            sems[e] = ctx.enter_context(nc.semaphore(f"sem_{e}"))

        blk = ctx.enter_context(nc.Block())

        g1_bf = bf16(RD, 0, 2048)
        wg1_bf = bf16(RD, 1024, 2048)
        wg2_bf = bf16(RD, 2048, 2048)
        s_all = f32(RD, 3072, 2048)
        ones_bf = bf16(RD, 5120, 128)
        ident_bf = bf16(RD, 5184, 128)
        sel_col = f32(RD, 5248, 1)
        cmask_col = f32(RD, 5249, 1)
        epsc = f32(RD, 5250, 2)
        nw_c = f32(RD, 5252, 16)
        b2_c = f32(RD, 5268, 16)
        b1_c = f32(RD, 5284, 64)
        sbf_ring = [bf16(RD, 5348 + 256 * i, 512) for i in range(2)]
        gw_bf = bf16(RD, 5860, 128)

        cc_sb = f32(RC, 0, 2048)
        ss_sb = f32(RC, 2048, 2048)
        wslab2 = [bf16(RC, 4096 + 4096 * i, 8192) for i in range(2)]
        dt_sb = bf16(RB, 9216, 2048)
        dkt_sb = bf16(RB, 10240, 2048)
        dq_sb = bf16(RB, 11264, 2048)
        wo_slab = [bf16(RC, i * 4096, 8192) for i in range(2)]
        p1_slab = [bf16(RC, i * 2048, 4096) for i in range(2)]
        p2_slab = [bf16(RC, 4096 + i * 4096, 8192) for i in range(2)]

        qs_ring = [bf16(RE, 256 * i, 512) for i in range(4)]
        qw_ring = [bf16(RE, 1024 + 256 * i, 512) for i in range(4)]
        sd_ring = [bf16(RE, 2048 + 256 * i, 512) for i in range(2)]
        kd_ring = [bf16(RE, 2560 + 256 * i, 512) for i in range(2)]
        qd_ring = [bf16(RE, 3072 + 256 * i, 512) for i in range(2)]
        oraw_stage = [bf16(RE, 3584 + i * 256, 512) for i in range(2)]
        qdg_stage = [bf16(RE, 4096 + i * 256, 512) for i in range(2)]
        a_t2r = [f32(RE, 3584 + 512 * i, 512) for i in range(2)]
        m_row = f32(RE, 0, 512)[0:1, :]
        msq_row = f32(RE, 512, 512)[0:1, :]
        var_row = f32(RE, 1024, 512)[0:1, :]
        ln_row = f32(RE, 1536, 512)[0:1, :]
        rstd_f = f32(RE, 2048, 512)[0:1, :]
        rstd_bf_row = bf16(RE, 2560, 512)[0:1, :]
        nmr_bf_row = bf16(RE, 2816, 512)[0:1, :]
        r2_sb = f32(RE, 3072, 512)
        s_prev_bf = bf16(RE, 4608, 2048)
        rstd_rows = bf16(RE, 0, 8192)[0:1, :]
        s_stage = [bf16(RE, 4096 + i * 256, 512) for i in range(2)]
        lnrow_ring = [f32(RE, 512 * i, 512)[0:1, :] for i in range(2)]
        rstd_ring = [bf16(RE, 1024 + 256 * i, 512)[0:1, :] for i in range(2)]
        s_tile_sb = bf16(RE, 0, 8192)
        t_comb = f32(RE, 4096, 512)

        xt_ring = [bf16(RB, i * 4096, 8192) for i in range(2)]
        sg_bf = bf16(RB, 0, 8192)
        qdg_blk = [bf16(RB, i * 1024, 2048) for i in range(4)]
        oraw_blk = [bf16(RB, 4096 + i * 1024, 2048) for i in range(4)]
        contrib_sb = f32(RB, 8192, 2048)
        sprev_f = f32(RB, 10240, 2048)
        ms_rows = bf16(RB, 8192, 8192)[0:1, :]
        xn_e = bf16(RB, 0, 8192)
        xe_ring = [f32(RB, 4096 + i * 512, 512) for i in range(2)]
        oute_ring = [f32(RB, 5120 + i * 512, 512) for i in range(2)]
        g_ring = [bf16(RB, 6144 + i * 256, 512) for i in range(2)]

        xsq_a = bf16(RA, 0, 8192)
        q_rot = bf16(RA, 0, 8192)
        k_rot = bf16(RA, 4096, 8192)
        v_buf = bf16(RA, 8192, 8192)

        o_full_r = [bf16(RA, 4096 * i, 8192) for i in range(2)]
        o_n_bf = bf16(RA, 8192, 8192)
        xsq_d = bf16(RA, 12288, 8192)
        h1_bf = bf16(RA, 0, 32768)

        E = sch.emit

        def dma(dst, src, reads=(), writes=(), stream="dmisc", q="sp"):
            E(q, lambda e, d=dst, s=src: e.dma_start(out=d, in_=s),
              reads=reads, writes=writes, sem=sch.sem(stream))

        pre_keys = ["cc", "ss", "dt", "dkt", "dq", "sel", "cmask", "epsc",
                    "nw", "b2", "b1"]
        dma(cc_sb, cc_t, writes=["cc"], stream="pre")
        dma(ss_sb, ss_t, writes=["ss"], stream="pre")
        dma(dt_sb, dt_t, writes=["dt"], stream="pre")
        dma(dkt_sb, dkt_t, writes=["dkt"], stream="pre")
        dma(dq_sb, dq_t, writes=["dq"], stream="pre")
        dma(sel_col, sel_in, writes=["sel"], stream="pre")
        dma(cmask_col, cmask_in, writes=["cmask"], stream="pre")
        dma(epsc, epsc_in, writes=["epsc"], stream="pre")
        dma(nw_c, nw_col, writes=["nw"], stream="pre")
        dma(b2_c, b2_col, writes=["b2"], stream="pre")
        dma(b1_c, b1_col, writes=["b1"], stream="pre")
        sch.bump_writes_to_total(pre_keys, "pre")
        pre2_keys = ["ones", "ident", "wg1", "wg2", "gw"]
        dma(ones_bf, ones_in, writes=["ones"], stream="pre2", q="pool")
        dma(ident_bf, ident_in, writes=["ident"], stream="pre2", q="pool")
        dma(wg1_bf.rearrange("p (k c) -> p k c", c=128),
            wg1.rearrange("(k p) c -> p k c", p=128), writes=["wg1"], stream="pre2", q="pool")
        dma(wg2_bf, wg2, writes=["wg2"], stream="pre2", q="pool")
        dma(gw_bf.rearrange("p (k c) -> p k c", c=8),
            gate_w_in.rearrange("(k p) c -> p k c", p=128), writes=["gw"], stream="pre2", q="pool")
        sch.bump_writes_to_total(pre2_keys, "pre2")

        for i in range(2):
            dma(wo_bf_d[i * 1024:(i + 1) * 1024, :], wo[i * 1024:(i + 1) * 1024, :],
                writes=["wo_bf"], stream="wc0", q="pool")
        sch.bump_writes_to_total(["wo_bf"], "wc0")

        def emit_p1_cast(lo, hi):
            for i in range(lo, hi):
                gate = ["cc_out"] if i < 2 else ["s_d:15:0"]
                dma(p1t_bf_d[i * 512:(i + 1) * 512, :], p1t[i * 512:(i + 1) * 512, :],
                    reads=gate, writes=["p1_bf"], stream="wc1", q="pool")
            sch.bump_writes_to_total(["p1_bf"], "wc1")

        def emit_p2_cast():
            for oc in range(KC):
                dma(p2t_bf_d[oc * H1:(oc + 1) * H1, :], p2t[oc * H1:(oc + 1) * H1, :],
                    reads=["s_d:15:1"], writes=[f"p2c:{oc}"], stream=f"wc2{oc}", q="pool")

        inv_d = 1.0 / float(D)
        xbf_ring = [bf16(RB, 4096 * i, 8192) for i in range(2)]
        xqv = xsq_a.rearrange("p (k t) -> p k t", t=TT)
        wg1v = wg1_bf.rearrange("p (k c) -> p k c", c=128)
        for tt in range(NT + 1):
            if tt < NT:
                dma(xbf_ring[tt % 2].rearrange("p (k t) -> p k t", t=TT),
                    x_fm.rearrange("(k p) t -> p k t", p=128)[:, :, tt * TT:(tt + 1) * TT],
                    writes=[f"xbf{tt % 2}"], stream=f"xbf{tt % 2}", q="pool")
            if tt == 0:
                continue
            t0 = (tt - 1) * TT
            xbk = f"xbf{(tt - 1) % 2}"
            xbv = xbf_ring[(tt - 1) % 2].rearrange("p (k t) -> p k t", t=TT)
            xbf_flat = xbf_ring[(tt - 1) % 2]
            E("dve", lambda e, x_=xbf_flat: e.tensor_tensor(
                out=xsq_a, in0=x_, in1=x_, op=ALU.mult),
              reads=[xbk], writes=["xsq"])
            for k in range(KC):
                E("pe", lambda e, k=k, xbv=xbv: e.matmul(pb[0][0:1, :], ones_bf[:, 0:1],
                                                         xbv[:, k, :],
                                                         start=(k == 0), stop=(k == KC - 1)),
                  reads=[xbk, "ones"], writes=["ps0"], inc=(k == KC - 1))
            for k in range(KC):
                E("pe", lambda e, k=k: e.matmul(pb[1][0:1, :], ones_bf[:, 0:1], xqv[:, k, :],
                                                start=(k == 0), stop=(k == KC - 1)),
                  reads=["xsq", "ones"], writes=["ps1"], inc=(k == KC - 1))
            E("dve", lambda e: e.tensor_scalar(out=m_row, in0=pb[0][0:1, :], scalar1=inv_d,
                                               scalar2=None, op0=ALU.mult),
              reads=["ps0"], writes=["mrow"])
            E("pool", lambda e: e.tensor_tensor(out=msq_row, in0=m_row, in1=m_row, op=ALU.mult),
              reads=["mrow"], writes=["msq"])
            E("dve", lambda e: e.scalar_tensor_tensor(out=var_row, in0=pb[1][0:1, :], scalar=inv_d,
                                                      in1=msq_row, op0=ALU.mult, op1=ALU.subtract),
              reads=["ps1", "msq"], writes=["var"])
            E("act", lambda e: e.activation(out=ln_row, in_=var_row, func=AF.Ln,
                                            bias=epsc[0:1, 0:1], scale=1.0),
              reads=["var", "epsc"], writes=["lnr"])
            E("act", lambda e: e.activation(out=rstd_f, in_=ln_row, func=AF.Exp, scale=-0.5),
              reads=["lnr"], writes=["rstdf"])
            E("dve", lambda e: e.tensor_copy(out=rstd_bf_row, in_=rstd_f),
              reads=["rstdf"], writes=["rstdb"])
            E("pool", lambda e: e.tensor_tensor(out=nmr_bf_row, in0=m_row,
                                                in1=rstd_f, op=ALU.mult),
              reads=["mrow", "rstdf"], writes=["nmrb"])
            E("pe", lambda e: e.matmul(pb[2][:], ones_bf[0:1, :], rstd_bf_row, start=True, stop=True),
              reads=["rstdb", "ones"], writes=["ps2"])
            E("pe", lambda e: e.matmul(pb[3][:], ones_bf[0:1, :], nmr_bf_row, start=True, stop=True),
              reads=["nmrb", "ones"], writes=["ps3"])
            E("dve", lambda e: e.tensor_copy(out=r2_sb, in_=pb[3][:]), reads=["ps3"], writes=["r2"])
            for kp in range(0, KC, 2):
                for k in (kp, kp + 1):
                    E("dve", lambda e, k=k, xbv=xbv: e.tensor_tensor(
                        out=a_t2r[k % 2], in0=xbv[:, k, :], in1=pb[2][:], op=ALU.mult),
                      reads=[xbk, "ps2"], writes=[f"t2{k % 2}"])
                for k in (kp, kp + 1):
                    E("dve", lambda e, k=k, xbv=xbv: e.tensor_tensor(
                        out=xbv[:, k, :], in0=a_t2r[k % 2], in1=r2_sb, op=ALU.subtract),
                      reads=[f"t2{k % 2}", "r2"], writes=[xbk])
            dma(xn_d.rearrange("(k p) t -> p k t", p=128)[:, :, t0:t0 + TT], xbv,
                reads=[xbk], writes=[f"xn_d:{tt - 1}"], stream="xnst", q="act")
            for k in range(KC):
                E("pe", lambda e, k=k, xbv=xbv: e.matmul(pb[4][:], wg1v[:, k, :], xbv[:, k, :],
                                                         start=(k == 0), stop=(k == KC - 1)),
                  reads=[xbk, "wg1"], writes=["ps4"], inc=(k == KC - 1))
            E("act", lambda e, t0=t0: e.copy(out=g1_bf[:, t0:t0 + TT], in_=pb[4][:]),
              reads=["ps4"], writes=["g1"])
            gwv = gw_bf.rearrange("p (k c) -> p k c", c=8)
            for k in range(KC):
                E("pe", lambda e, k=k, xbv=xbv: e.matmul(pb[5][0:8, :], gwv[:, k, :], xbv[:, k, :],
                                                         start=(k == 0), stop=(k == KC - 1)),
                  reads=[xbk, "gw"], writes=["ps5"], inc=(k == KC - 1))

        sch.phase_barrier()

        qrv = q_rot.rearrange("p (h t) -> p h t", t=TPC)
        krv = k_rot.rearrange("p (h t) -> p h t", t=TPC)
        vbv = v_buf.rearrange("p (c e) -> p c e", e=512)

        def qk_instance(wsv, wslk, xtv, xk, tt, hh, dstv, dkey, bank, emit_qdg_g):
            t0 = tt * TT
            ring = (tt * GH + hh) % 4
            q_s = qs_ring[ring]
            q_w = qw_ring[ring]
            qsk2 = f"q_s{ring}"
            qwk2 = f"q_w{ring}"
            for k in range(KC):
                E("pe", lambda e, k=k, hh=hh, bank=bank, xtv=xtv, wsv=wsv: e.matmul(
                    pb[bank][:], wsv[:, k, hh * 128:(hh + 1) * 128], xtv[:, k, :],
                    start=(k == 0), stop=(k == KC - 1)),
                  reads=[xk, wslk], writes=[f"ps{bank}"], inc=(k == KC - 1))
            E("act", lambda e, bank=bank, q_s=q_s: e.activation(
                out=q_s, in_=pb[bank][:], func=AF.Silu),
              reads=[f"ps{bank}"], writes=[qsk2])
            E("act", lambda e, q_s=q_s, q_w=q_w: e.copy(
                out=q_w[0:64, :], in_=q_s[64:128, :]),
              reads=[qsk2], writes=[qwk2])
            E("act", lambda e, q_s=q_s, q_w=q_w: e.copy(
                out=q_w[64:128, :], in_=q_s[0:64, :]),
              reads=[qsk2], writes=[qwk2])
            E("pool", lambda e, t0=t0, q_w=q_w: e.tensor_tensor(
                out=q_w, in0=q_w, in1=ss_sb[:, t0:t0 + TT], op=ALU.mult),
              reads=[qwk2, "ss"], writes=[qwk2])
            E("dve", lambda e, t0=t0, q_s=q_s: e.tensor_tensor(
                out=q_s, in0=q_s, in1=cc_sb[:, t0:t0 + TT], op=ALU.mult),
              reads=[qsk2, "cc"], writes=[qsk2])
            E("dve", lambda e, hh=hh, t0=t0, dstv=dstv, q_s=q_s, q_w=q_w:
                e.tensor_tensor(out=dstv[:, hh, t0:t0 + TT], in0=q_s, in1=q_w,
                                op=ALU.add),
              reads=[qsk2, qwk2], writes=[dkey])

        for g in range(NG):
            for hh in range(GH):
                E("dve", lambda e, g=g, hh=hh: e.memset(
                    s_all[:, g * 512 + hh * 128:(g * 512 + (hh + 1) * 128)], 0.0),
                  writes=[f"S:{g}:{hh}"])
            E("dve", lambda e: e.memset(sbf_ring[0], 0.0), writes=["sbf0"])
            E("dve", lambda e: e.memset(sbf_ring[1], 0.0), writes=["sbf1"])

            wsl_q = wslab2[0].rearrange("p (k c) -> p k c", c=512)
            wsl_k = wslab2[1].rearrange("p (k c) -> p k c", c=512)
            wsl_v = wsl_q
            dma(wsl_q, wq.rearrange("(k p) c -> p k c", p=128)[:, :, g * 512:(g + 1) * 512],
                writes=["wsl0"], stream="wsl0", q="pool")
            dma(wsl_k, wk.rearrange("(k p) c -> p k c", p=128)[:, :, g * 512:(g + 1) * 512],
                writes=["wsl1"], stream="wsl1", q="pool")

            for tt in range(NT):
                xt = xt_ring[tt % 2]
                xk = f"xt:{tt % 2}"
                xtv = xt.rearrange("p (k t) -> p k t", t=TT)
                dma(xtv, xn_d.rearrange("(k p) t -> p k t", p=128)[:, :, tt * TT:(tt + 1) * TT],
                    reads=[f"xn_d:{tt}"], writes=[xk], stream=f"xt{tt % 2}")
                for hh in range(GH):
                    qk_instance(wsl_q, "wsl0", xtv, xk, tt, hh, qrv, "q_rot",
                                (tt * GH + hh) % 4, g)
            dma(wsl_v, wv.rearrange("(k p) c -> p k c", p=128)[:, :, g * 512:(g + 1) * 512],
                writes=["wsl0"], stream="wsl0", q="pool")

            def emit_sT_kT(c):
                psA = pb[2 + c % 2]
                psB = pbf[c % 2]
                kA = f"ps{2 + c % 2}"
                kB = f"psb{c % 2}"
                for hh in range(GH):
                    E("pe", lambda e, hh=hh, c=c, psA=psA: e.matmul(
                        psA[:, hh * 128:(hh + 1) * 128],
                        krv[:, hh, c * 128:(c + 1) * 128],
                        qrv[:, hh, c * 128:(c + 1) * 128],
                        start=True, stop=True),
                      reads=["k_rot", "q_rot"], writes=[kA], inc=(hh == GH - 1))
                for hh in range(GH):
                    E("pe", lambda e, hh=hh, c=c, psB=psB: e.transpose(
                        psB[:, hh * 128:(hh + 1) * 128],
                        krv[:, hh, c * 128:(c + 1) * 128], ident_bf),
                      reads=["k_rot", "ident"], writes=[kB], inc=(hh == GH - 1))

            for tt in range(NT):
                xt = xt_ring[tt % 2]
                xk = f"xt:{tt % 2}"
                xtv = xt.rearrange("p (k t) -> p k t", t=TT)
                dma(xtv, xn_d.rearrange("(k p) t -> p k t", p=128)[:, :, tt * TT:(tt + 1) * TT],
                    reads=[f"xn_d:{tt}"], writes=[xk], stream=f"xt{tt % 2}")
                for hh in range(GH):
                    qk_instance(wsl_k, "wsl1", xtv, xk, tt, hh, krv, "k_rot",
                                (tt * GH + hh) % 2, g)
                for tc in range(4):
                    tcg = tt * 4 + tc
                    bank = (tc % 2)
                    for k in range(KC):
                        E("pe", lambda e, k=k, tc=tc, bank=bank, xtv=xtv: e.matmul(
                            pb[bank][:], xtv[:, k, tc * 128:(tc + 1) * 128], wsl_v[:, k, :],
                            start=(k == 0), stop=(k == KC - 1)),
                          reads=[xk, "wsl0"], writes=[f"ps{bank}"], inc=(k == KC - 1))
                    E("act", lambda e, tcg=tcg, bank=bank: e.copy(out=vbv[:, tcg, :],
                                                                  in_=pb[bank][:]),
                      reads=[f"ps{bank}"], writes=["v_buf"])
                emit_sT_kT(tt * 4)
                for c in range(tt * 4, tt * 4 + 4):
                    psA = pb[2 + c % 2]
                    psC = pb[4 + c % 2]
                    psD = pb[2 + c % 2]
                    psB = pbf[c % 2]
                    kA = f"ps{2 + c % 2}"
                    kCk = f"ps{4 + c % 2}"
                    kD = f"ps{2 + c % 2}"
                    kB = f"psb{c % 2}"
                    sd_bf = sd_ring[c % 2]
                    kd_bf = kd_ring[c % 2]
                    qd_bf = qd_ring[c % 2]
                    sdk, kdk, qdk = f"sd{c % 2}", f"kd{c % 2}", f"qd{c % 2}"
                    sdv = sd_bf.rearrange("p (h t) -> p h t", t=128)
                    qdv = qd_bf.rearrange("p (h t) -> p h t", t=128)
                    kdv = kd_bf.rearrange("p (h e) -> p h e", e=128)
                    sbf_in = sbf_ring[c % 2]
                    sbf_out = sbf_ring[(c + 1) % 2]
                    sbk_in, sbk_out = f"sbf{c % 2}", f"sbf{(c + 1) % 2}"
                    sbv = sbf_in.rearrange("p (h e) -> p h e", e=128)
                    E("dve", lambda e, g=g, psA=psA, sd_bf=sd_bf: e.tensor_tensor(
                        out=sd_bf, in0=psA[:], in1=dt_sb[:, g * 512:(g + 1) * 512],
                        op=ALU.mult),
                      reads=[kA, "dt"], writes=[sdk])
                    E("dve", lambda e, g=g, psB=psB, kd_bf=kd_bf: e.tensor_tensor(
                        out=kd_bf, in0=psB[:], in1=dkt_sb[:, g * 512:(g + 1) * 512],
                        op=ALU.mult),
                      reads=[kB, "dkt"], writes=[kdk])
                    E("pool", lambda e, g=g, c=c, qdv=qdv: e.tensor_tensor(
                        out=qdv[:, :, :],
                        in0=qrv[:, :, c * 128:(c + 1) * 128],
                        in1=dq_sb[:, g * 512:(g + 1) * 512].rearrange(
                            "p (h t) -> p h t", t=128),
                        op=ALU.mult),
                      reads=["q_rot", "dq"], writes=[qdk])
                    if c + 1 <= tt * 4 + 3:
                        emit_sT_kT(c + 1)
                    for hh in range(GH):
                        E("pe", lambda e, hh=hh, c=c, psD=psD, kdv=kdv: e.matmul(
                            psD[:, hh * 128:(hh + 1) * 128],
                            kdv[:, hh, :], vbv[:, c, hh * 128:(hh + 1) * 128],
                            start=True, stop=True),
                          reads=[kdk, "v_buf"], writes=[kD], inc=(hh == GH - 1))
                    for hh in range(GH):
                        E("pe", lambda e, hh=hh, c=c, psC=psC, sdv=sdv: e.matmul(
                            psC[:, hh * 128:(hh + 1) * 128],
                            vbv[:, c, hh * 128:(hh + 1) * 128],
                            sdv[:, hh, :],
                            start=True, stop=False),
                          reads=["v_buf", sdk], writes=[kCk], inc=False)
                        E("pe", lambda e, hh=hh, psC=psC, qdv=qdv, sbv=sbv: e.matmul(
                            psC[:, hh * 128:(hh + 1) * 128],
                            sbv[:, hh, :], qdv[:, hh, :],
                            start=False, stop=True),
                          reads=[sbk_in, qdk], writes=[kCk], inc=(hh == GH - 1))
                    for hh in range(GH):
                        h = g * GH + hh
                        E("dve", lambda e, hh=hh, h=h, g=g, psD=psD: e.scalar_tensor_tensor(
                            out=s_all[:, g * 512 + hh * 128: g * 512 + (hh + 1) * 128],
                            in0=s_all[:, g * 512 + hh * 128: g * 512 + (hh + 1) * 128],
                            scalar=lam_c[h],
                            in1=psD[:, hh * 128:(hh + 1) * 128],
                            op0=ALU.mult, op1=ALU.add),
                          reads=[kD, f"S:{g}:{hh}"], writes=[f"S:{g}:{hh}"])
                    E("act", lambda e, g=g, sbf_out=sbf_out: e.copy(
                        out=sbf_out, in_=s_all[:, g * 512:(g + 1) * 512]),
                      reads=[f"S:{g}:0", f"S:{g}:1", f"S:{g}:2", f"S:{g}:3"],
                      writes=[sbk_out])
                    qst = qdg_stage[c % 2]
                    qsk = f"qdgs:{c % 2}"
                    for hh in range(GH):
                        h = g * GH + hh
                        lam_pw = float(np.float64(lam_c[h]) ** c)
                        E("dve", lambda e, hh=hh, lam_pw=lam_pw, qst=qst, qdv=qdv:
                            e.tensor_scalar(
                                out=qst.rearrange("p (h t) -> p h t", t=128)[:, hh, :],
                                in0=qdv[:, hh, :], scalar1=lam_pw, scalar2=None,
                                op0=ALU.mult),
                          reads=[qdk], writes=[qsk])
                    dma(qdg_d.rearrange("(a p) (b t) -> p a b t", p=128, t=128)
                        [:, g * NCH + c, :, :],
                        qst.rearrange("p (h t) -> p h t", t=128),
                        reads=[qsk], writes=[f"qdg_d:{g}:{c}"], stream=f"qdgst{c % 2}",
                        q="act")
                    ost = oraw_stage[c % 2]
                    osk = f"oraws:{c % 2}"
                    E("act", lambda e, psC=psC, ost=ost: e.copy(out=ost, in_=psC[:]),
                      reads=[kCk], writes=[osk])
                    dma(oraw_d[(g * NCH + c) * HD:(g * NCH + c + 1) * HD, :], ost,
                        reads=[osk], writes=[f"oraw_d:{g}:{c}"], stream=f"orawst{c % 2}",
                        q="act")

        E("dve", lambda e: e.tensor_scalar(out=contrib_sb, in0=s_all, scalar1=cmask_col,
                                           scalar2=None, op0=ALU.mult),
          reads=[f"S:{g_}:{h_}" for g_ in range(4) for h_ in range(4)] + ["cmask"],
          writes=["contrib"])
        dma(cc_in_d.rearrange("(h p) e -> p h e", p=128),
            contrib_sb.rearrange("p (h e) -> p h e", e=128),
            reads=["contrib"], writes=["cc_in"], stream="ccin", q="act")
        E("pool", lambda e: e.collective_compute(
            "AllReduce", ALU.add,
            replica_groups=[[0, 1], [2, 3], [4, 5], [6, 7]],
            ins=[cc_in_d], outs=[cc_out_d]),
          reads=["cc_in"], writes=["cc_out"], sem="coll")
        emit_p1_cast(0, 4)
        dma(sprev_f.rearrange("p (h e) -> p h e", e=128),
            cc_out_d.rearrange("(h p) e -> p h e", p=128),
            reads=["cc_out"], writes=["sprevf", "dkt", "dq"], stream="sprevf")
        E("dve", lambda e: e.tensor_scalar(out=s_prev_bf, in0=sprev_f, scalar1=sel_col,
                                           scalar2=None, op0=ALU.mult),
          reads=["sprevf", "sel"], writes=["sprev"])
        emit_p2_cast()

        onv = o_n_bf.rearrange("p (h t) -> p h t", t=TT)
        sgv = sg_bf.rearrange("p (h t) -> p h t", t=TT)
        xqvd = xsq_d.rearrange("p (h t) -> p h t", t=TT)
        spv = s_prev_bf.rearrange("p (h e) -> p h e", e=128)
        orv = oraw_d.rearrange("(a p) (b t) -> p a b t", p=128, t=128)
        qrv_d = qdg_d.rearrange("(a p) (b t) -> p a b t", p=128, t=128)
        for tt in range(NT):
            t0 = tt * TT
            o_full = o_full_r[tt % 2]
            ofk = f"of{tt % 2}"
            ofv = o_full.rearrange("p (h t) -> p h t", t=TT)
            for g in range(NG):
                ri = (tt * NG + g) % 4
                qblk = qdg_blk[ri]
                qbk = f"qdgb:{ri}"
                oblk = oraw_blk[ri]
                obk = f"orawb:{ri}"
                rblk = [f"qdg_d:{g}:{tt * 4 + i}" for i in range(4)]
                rblk2 = [f"oraw_d:{g}:{tt * 4 + i}" for i in range(4)]
                dma(qblk.rearrange("p (c w) -> p c w", w=512),
                    qdg_d.rearrange("(a p) w -> p a w", p=128)
                    [:, g * NCH + tt * 4: g * NCH + tt * 4 + 4, :],
                    reads=rblk, writes=[qbk], stream=f"qdgl{ri}")
                dma(oblk.rearrange("p (c w) -> p c w", w=512),
                    oraw_d.rearrange("(a p) w -> p a w", p=128)
                    [:, g * NCH + tt * 4: g * NCH + tt * 4 + 4, :],
                    reads=rblk2, writes=[obk], stream=f"orawl{ri}", q="act")
                qbv = qblk.rearrange("p (c hh t) -> p c hh t", hh=GH, t=128)
                obv = oblk.rearrange("p (c hh t) -> p c hh t", hh=GH, t=128)
                for hh in range(GH):
                    h = g * GH + hh
                    bank = 2 + h % 2
                    E("pe", lambda e, h=h, hh=hh, qbv=qbv, bank=bank: e.matmul(
                        pb[bank][:], spv[:, h, :], qbv[:, :, hh, :],
                        start=True, stop=True),
                      reads=["sprev", qbk], writes=[f"ps{bank}"])
                    E("dve", lambda e, h=h, hh=hh, obv=obv, bank=bank, ofv=ofv:
                        e.scalar_tensor_tensor(
                            out=ofv[:, h, :], in0=obv[:, :, hh, :], scalar=1.0,
                            in1=pb[bank][:].rearrange("p (c t) -> p c t", t=128),
                            op0=ALU.mult, op1=ALU.add),
                      reads=[obk, f"ps{bank}"], writes=[ofk])
            for h in range(HEADS):
                gbank = 4 + h % 2
                E("pe", lambda e, h=h, t0=t0, gbank=gbank: e.matmul(
                    pb[gbank][:], wg2_bf[:, h * 128:(h + 1) * 128], g1_bf[:, t0:t0 + TT],
                    start=True, stop=True),
                  reads=["wg2", "g1"], writes=[f"ps{gbank}"])
                E("act", lambda e, h=h, gbank=gbank: e.activation(
                    out=sgv[:, h, :], in_=pb[gbank][:], func=AF.Sigmoid),
                  reads=[f"ps{gbank}"], writes=["sg"])
            E("dve", lambda e, o_full=o_full: e.tensor_tensor(
                out=o_full, in0=o_full, in1=sg_bf, op=ALU.mult),
              reads=[ofk, "sg"], writes=[ofk])
            E("act", lambda e, o_full=o_full: e.activation(
                out=xsq_d, in_=o_full, func=AF.Square),
              reads=[ofk], writes=["xsqd"])
            for h in range(HEADS):
                mbank = 4 + h % 2
                E("pe", lambda e, h=h, mbank=mbank: e.matmul(
                    pb[mbank][0:1, :], ones_bf[:, 0:1], xqvd[:, h, :], start=True, stop=True),
                  reads=["xsqd", "ones"], writes=[f"ps{mbank}"])
                E("act", lambda e, h=h, mbank=mbank: e.copy(
                    out=ms_rows[:, h * 512:(h + 1) * 512], in_=pb[mbank][0:1, :]),
                  reads=[f"ps{mbank}"], writes=["msrow"])
            for half in range(2):
                E("act", lambda e, half=half: e.activation(
                    out=rstd_rows[:, half * 4096:(half + 1) * 4096],
                    in_=ms_rows[:, half * 4096:(half + 1) * 4096], func=AF.Ln,
                    bias=epsc[0:1, 1:2], scale=1.0 / HD),
                  reads=["msrow", "epsc"], writes=["rstdrow"])
            for half in range(2):
                E("act", lambda e, half=half: e.activation(
                    out=rstd_rows[:, half * 4096:(half + 1) * 4096],
                    in_=rstd_rows[:, half * 4096:(half + 1) * 4096], func=AF.Exp, scale=-0.5),
                  reads=["rstdrow"], writes=["rstdrow"])
            for h in range(HEADS):
                mbank = 4 + h % 2
                E("pe", lambda e, h=h, mbank=mbank: e.matmul(
                    pb[mbank][:], ones_bf[0:1, :], rstd_rows[:, h * 512:(h + 1) * 512],
                    start=True, stop=True),
                  reads=["rstdrow", "ones", f"ps{mbank}"], writes=[f"ps{mbank}"])
                E("dve", lambda e, h=h, mbank=mbank, ofv=ofv: e.scalar_tensor_tensor(
                    out=onv[:, h, :], in0=ofv[:, h, :], scalar=nw_c[:, h:h + 1],
                    in1=pb[mbank][:], op0=ALU.mult, op1=ALU.mult),
                  reads=[ofk, "nw", f"ps{mbank}"], writes=["o_n"])
            for oq in range(4):
                wsl = wo_slab[oq % 2]
                wslk = f"wosl:{oq % 2}"
                dma(wsl.rearrange("p (k c) -> p k c", c=512),
                    wo_bf_d.rearrange("(k p) c -> p k c", p=128)[:, :, oq * 512:(oq + 1) * 512],
                    reads=["wo_bf"], writes=[wslk], stream=f"wosl{oq % 2}")
                wslv = wsl.rearrange("p (k c) -> p k c", c=512)
                for oc in range(4):
                    bank = oc % 2
                    for k in range(KC):
                        E("pe", lambda e, k=k, oc=oc, bank=bank, wslv=wslv: e.matmul(
                            pb[bank][:], wslv[:, k, oc * 128:(oc + 1) * 128], onv[:, k, :],
                            start=(k == 0), stop=(k == KC - 1)),
                          reads=[wslk, "o_n"], writes=[f"ps{bank}"], inc=(k == KC - 1))
                    sst = s_stage[oc % 2]
                    sstk = f"sstg:{oc % 2}"
                    E("act", lambda e, bank=bank, sst=sst: e.copy(out=sst, in_=pb[bank][:]),
                      reads=[f"ps{bank}"], writes=[sstk])
                    o = oq * 4 + oc
                    dma(s_d.rearrange("(o p) t -> p o t", p=128)[:, o, t0:t0 + TT], sst,
                        reads=[sstk], writes=[f"s_d:{o}:{tt}"], stream=f"sst{oc % 2}", q="act")

        sch.phase_barrier()

        h1v = h1_bf.rearrange("p (m t) -> p m t", t=TT)
        xnev = xn_e.rearrange("p (k t) -> p k t", t=TT)
        stv = s_tile_sb.rearrange("p (o t) -> p o t", t=TT)
        for tt in range(NT):
            t0 = tt * TT
            dma(xnev, xn_d.rearrange("(k p) t -> p k t", p=128)[:, :, t0:t0 + TT],
                reads=[f"xn_d:{tt}"], writes=["xn_e"], stream="xne")
            dma(stv, s_d.rearrange("(o p) t -> p o t", p=128)[:, :, t0:t0 + TT],
                reads=[f"s_d:{o_}:{tt}" for o_ in range(KC)], writes=["s_e"], stream="se")
            for ms in range(32):
                sl = p1_slab[ms % 2]
                slk = f"p1s:{ms % 2}"
                dma(sl.rearrange("p (k c) -> p k c", c=256),
                    p1t_bf_d.rearrange("(k p) c -> p k c", p=128)[:, :, ms * 256:(ms + 1) * 256],
                    reads=["p1_bf"], writes=[slk], stream=f"p1s{ms % 2}")
                slv = sl.rearrange("p (k c) -> p k c", c=256)
                for mi in range(2):
                    m = ms * 2 + mi
                    bank = m % 4
                    for k in range(KC):
                        E("pe", lambda e, k=k, mi=mi, bank=bank, slv=slv: e.matmul(
                            pb[bank][:], slv[:, k, mi * 128:(mi + 1) * 128], xnev[:, k, :],
                            start=(k == 0), stop=(k == KC - 1)),
                          reads=[slk, "xn_e"], writes=[f"ps{bank}"], inc=(k == KC - 1))
                    E("act", lambda e, m=m, bank=bank: e.activation(
                        out=h1v[:, m, :], in_=pb[bank][:], func=AF.Gelu,
                        bias=b1_c[:, m:m + 1], scale=1.0),
                      reads=[f"ps{bank}", "b1"], writes=["h1"])
            for oc in range(KC):
                sl = p2_slab[oc % 2]
                slk = f"p2s:{oc % 2}"
                dma(sl.rearrange("p (m c) -> p m c", c=128),
                    p2t_bf_d[oc * H1:(oc + 1) * H1, :].rearrange("(m p) c -> p m c", p=128),
                    reads=[f"p2c:{oc}"], writes=[slk], stream=f"p2s{oc % 2}")
                slv = sl.rearrange("p (m c) -> p m c", c=128)
                bank = 4 + oc % 2
                for m in range(MC):
                    E("pe", lambda e, m=m, bank=bank, slv=slv: e.matmul(
                        pb[bank][:], slv[:, m, :], h1v[:, m, :],
                        start=(m == 0), stop=(m == MC - 1)),
                      reads=[slk, "h1"], writes=[f"ps{bank}"], inc=(m == MC - 1))
                gr = g_ring[oc % 2]
                grk = f"g_e:{oc % 2}"
                E("act", lambda e, oc=oc, bank=bank, gr=gr: e.activation(
                    out=gr, in_=pb[bank][:], func=AF.Silu, bias=b2_c[:, oc:oc + 1], scale=1.0),
                  reads=[f"ps{bank}", "b2"], writes=[grk])
                xr = xe_ring[oc % 2]
                xrk = f"x_e:{oc % 2}"
                dma(xr, x_fm[oc * 128:(oc + 1) * 128, t0:t0 + TT], writes=[xrk],
                    stream=f"xe{oc % 2}")
                E("dve", lambda e, oc=oc, gr=gr: e.tensor_tensor(
                    out=t_comb, in0=gr, in1=stv[:, oc, :], op=ALU.mult),
                  reads=[grk, "s_e"], writes=["t_comb"])
                our = oute_ring[oc % 2]
                ourk = f"out_e:{oc % 2}"
                E("dve", lambda e, xr=xr, our=our: e.tensor_tensor(
                    out=our, in0=t_comb, in1=xr, op=ALU.add),
                  reads=["t_comb", xrk], writes=[ourk])
                dma(y[oc * 128:(oc + 1) * 128, t0:t0 + TT], our,
                    reads=[ourk], writes=["y"], stream=f"ystr{oc % 2}", q="act")

        if debug:
            dxn = nc.dram_tensor("dxn", [D, TPC], BF16, kind="ExternalOutput").ap()
            dqdg = nc.dram_tensor("dqdg", [NG * NCH * HD, GH * CH], BF16, kind="ExternalOutput").ap()
            doraw = nc.dram_tensor("doraw", [NG * NCH * HD, GH * CH], BF16,
                                   kind="ExternalOutput").ap()
            dsd = nc.dram_tensor("dsd", [D, TPC], BF16, kind="ExternalOutput").ap()
            dg1 = nc.dram_tensor("dg1", [HD, TPC], BF16, kind="ExternalOutput").ap()
            dsprev = nc.dram_tensor("dsprev", [HD, HEADS * HD], BF16,
                                    kind="ExternalOutput").ap()
            dsall = nc.dram_tensor("dsall", [HD, HEADS * HD], F32,
                                   kind="ExternalOutput").ap()
            sch.phase_barrier()
            dma(dxn, xn_d, writes=["dbg"], stream="dbg")
            dma(dqdg, qdg_d, writes=["dbg"], stream="dbg")
            dma(doraw, oraw_d, writes=["dbg"], stream="dbg")
            dma(dsd, s_d, writes=["dbg"], stream="dbg")
            dma(dg1, g1_bf, reads=["g1"], writes=["dbg"], stream="dbg")
            dma(dsprev, s_prev_bf, reads=["sprev"], writes=["dbg"], stream="dbg")
            dma(dsall, s_all, reads=[f"S:{g_}:{h_}" for g_ in range(4) for h_ in range(4)],
                writes=["dbg"], stream="dbg")

        for t in list(sch.cnt):
            if t in ("pe", "act", "dve", "pool"):
                continue
            if sch.cnt[t] > 0:
                sch.barrier_wait("sp", t, sch.cnt[t])

        for name in sch.cnt:
            if name not in sems:
                sems[name] = ctx.enter_context(nc.semaphore(f"sem_{name}"))

        def run_queue(engine, q):
            for item in q:
                if item[0] == "wait":
                    engine.wait_ge(sems[item[1]], item[2])
                else:
                    _, fn, semname, step = item
                    inst = fn(engine)
                    if semname is not None:
                        inst.then_inc(sems[semname], step)

        @blk.tensor
        def _(e):
            run_queue(e, sch.queues["pe"])

        @blk.scalar
        def _(e):
            run_queue(e, sch.queues["act"])

        @blk.vector
        def _(e):
            run_queue(e, sch.queues["dve"])

        @blk.gpsimd
        def _(e):
            run_queue(e, sch.queues["pool"])

        @blk.sync
        def _(e):
            run_queue(e, sch.queues["sp"])

    return nc


_CACHE = {}


def kernel(debug=False, **inputs):
    x = np.asarray(inputs["L0_attn_x"], np.float32)
    wq = np.asarray(inputs["L0_attn_Wq"], np.float32)
    wk = np.asarray(inputs["L0_attn_Wk"], np.float32)
    wv = np.asarray(inputs["L0_attn_Wv"], np.float32)
    wo = np.asarray(inputs["L0_attn_Wo"], np.float32)
    wg1 = np.asarray(inputs["L0_attn_Wg1"], np.float32)
    wg2 = np.asarray(inputs["L0_attn_Wg2"], np.float32)
    nw = np.asarray(inputs["L0_attn_norm_weight"], np.float32)
    p1 = np.asarray(inputs["L0_ffn_proj1"], np.float32)
    b1 = np.asarray(inputs["L0_ffn_proj1_bias"], np.float32)
    p2 = np.asarray(inputs["L0_ffn_proj2"], np.float32)
    b2 = np.asarray(inputs["L0_ffn_proj2_bias"], np.float32)
    gate_w = np.asarray(inputs["L0_ffn_gate_w"], np.float32)

    cc, ss, dt, dkt, dq, tabg, _ = _tables()
    p1t = np.ascontiguousarray(p1.T)
    p2t = np.ascontiguousarray(
        p2.T.reshape(H1, KC, HD).transpose(1, 0, 2).reshape(KC * H1, HD))
    nw_col = np.ascontiguousarray(nw.reshape(HEADS, HD).T)
    b1_col = np.ascontiguousarray(b1.reshape(MC, HD).T)
    b2_col = np.ascontiguousarray(b2.reshape(KC, HD).T)
    ones_f = np.ones((HD, HD), np.float32)
    ident_f = np.eye(HD, dtype=np.float32)
    epsc = np.tile(np.array([EPS_LN, EPS_RMS], np.float32)[None, :], (HD, 1))

    ck = f"nc{int(debug)}"
    if ck not in _CACHE:
        _CACHE[ck] = build_program(debug=debug)
    nc = _CACHE[ck]

    in_maps = []
    for c in range(8):
        b, half = c // 2, c % 2
        x_fm = np.ascontiguousarray(x[b, half * TPC:(half + 1) * TPC, :].T)
        sel = np.full((HD, 1), 1.0 if half == 1 else 0.0, np.float32)
        cmask = np.full((HD, 1), 1.0 if half == 0 else 0.0, np.float32)
        in_maps.append({
            "x_fm": x_fm, "wq": wq, "wk": wk, "wv": wv, "wo": wo,
            "wg1": wg1, "wg2": wg2, "nw_col": nw_col,
            "p1t": p1t, "b1_col": b1_col, "p2t": p2t, "b2_col": b2_col,
            "cc_t": cc[half], "ss_t": ss[half],
            "dt_t": dt, "dkt_t": dkt, "dq_t": dq,
            "ones_in": ones_f, "ident_in": ident_f,
            "sel_in": sel, "cmask_in": cmask, "epsc_in": epsc,
            "gate_w_in": gate_w,
        })

    res = run_bass_kernel_spmd(nc, in_maps, core_ids=list(range(8)))

    out = np.empty((B, SEQ, D), np.float32)
    for c in range(8):
        b, half = c // 2, c % 2
        out[b, half * TPC:(half + 1) * TPC, :] = res.results[c]["y"].T
    if debug:
        return out, res.results
    return out
